# version 08
# baseline (speedup 1.0000x reference)
import math
import numpy as np
from contextlib import ExitStack

import concourse.bass as bass
import concourse.mybir as mybir
from concourse.bass_utils import run_bass_kernel_spmd

F32 = mybir.dt.float32
F16 = mybir.dt.float16
ALU = mybir.AluOpType
AF = mybir.ActivationFunctionType

B, T, D = 4, 2048, 2048
TOK = 1024
DCH = 16
NQH = 16
NGRP = 4
HID1 = 5632
NBC = 22
SCALE = float(np.float32(128.0) ** np.float32(-0.5))
NEG = -30000.0
NKTR = {0: (8, 4), 1: (8, 8)}

_STAGE = "full"
_TAPS = False
_FFNC = 16
_PHASE = ["?"]
_TENLBL = []


class Sem:
    def __init__(self, h):
        self.h = h
        self.n = 0

    def mark(self):
        return (self, self.n)


class Prog:
    ENGS = ("syn", "ten", "act", "dve", "gps")

    def __init__(self):
        self.q = {e: [] for e in self.ENGS}

    def emit(self, eng, builder, wait=None, waits=(), inc=None):
        if eng == "ten":
            _TENLBL.append(_PHASE[0])
        mk = None
        if inc is not None:
            s, k = inc
            s.n += k
            mk = (s, s.n)
        self.q[eng].append((builder, wait, tuple(w for w in waits if w is not None), inc))
        return mk

    def run(self, eng, engine):
        for builder, wait, waits, inc in self.q[eng]:
            for (s, v) in waits:
                if v > 0:
                    engine.wait_ge(s.h, v)
            ins = builder(engine)
            if wait is not None and wait[1] > 0:
                ins._wait_ge(wait[0].h, wait[1])
            if inc is not None:
                ins.then_inc(inc[0].h, inc[1])


class Bump:
    def __init__(self, ap_full, cols, name):
        self.t = ap_full
        self.cols = cols
        self.name = name
        self.off = 0

    def take(self, n):
        assert self.off + n <= self.cols, f"{self.name} overflow {self.off}+{n}>{self.cols}"
        o = self.off
        self.off += n
        return self.t[:, o:o + n]


def build_nc():
    nc = bass.Bass("TRN2", target_bir_lowering=False)
    P = Prog()

    def din(name, shape, dt=F32):
        return nc.dram_tensor(name, list(shape), dt, kind="ExternalInput").ap()

    x0t = din("x0t", (128, DCH * TOK))
    qidx_h = din("qidxb", (128, TOK), F16)
    kidx_h = din("kidx", (128, 16))
    cos_h = din("ropecos", (128, TOK))
    sin_h = din("ropesin", (128, TOK))
    bq_h = din("bq_pk", (128, 16))
    bk_h = din("bk_pk", (128, 4))
    bvb_h = din("bv_b", (128, 512))
    bo_h = din("bo_pk", (128, 16))
    b0_h = din("b0_pk", (128, 64))
    b1_h = din("b1_pk", (128, 16))
    wq0_h = din("wq0", (NQH, 128, 2048))
    wk0_h = din("wk0", (4, 128, 2048))
    wv0_h = din("wv0", (4, 128, 2048))
    wo0_h = din("wo0", (NQH, 128, D))
    w00_h = din("w00", (16, 2, 128, 4096))
    w10_h = din("w10", (16, 2, 128, 4096))
    wq1_h = din("wq1", (NQH, 128, 2048))
    wk1_h = din("wk1", (4, 128, 2048))
    wv1_h = din("wv1", (4, 128, 2048))
    wd1_h = din("wd1", (NQH, 128, D))
    w1q_h = din("w1q", (NBC, 2, 128, 2048))
    w2q_h = din("w2q", (NBC, 2, 128, 2048))
    w3q_h = din("w3q", (NBC, 2, 128, 2048))
    gsrc_h = din("gsrc", (66, 128, 512))

    out_h = nc.dram_tensor("out", [128, DCH * TOK], F32, kind="ExternalOutput").ap()
    taps = {}
    if _TAPS:
        taps["xnh"] = nc.dram_tensor("tap_xnh", [128, DCH * TOK], F16, kind="ExternalOutput").ap()
        taps["xnl"] = nc.dram_tensor("tap_xnl", [128, DCH * TOK], F16, kind="ExternalOutput").ap()
        taps["kv"] = nc.dram_tensor("tap_kv", [2, 2, NGRP, 128, 2048], F16, kind="ExternalOutput").ap()
        taps["d16"] = nc.dram_tensor("tap_d16", [128, 4096], F16, kind="ExternalOutput").ap()
        taps["thr"] = nc.dram_tensor("tap_thr", [128, 8], F32, kind="ExternalOutput").ap()
        taps["gp"] = nc.dram_tensor("tap_gp", [128, 72], F32, kind="ExternalOutput").ap()
        taps["ga"] = nc.dram_tensor("tap_ga", [128, 24], F32, kind="ExternalOutput").ap()

    kvs = [nc.dram_tensor(f"kv_send{L}", [2, NGRP, 128, 2048], F16, kind="Internal").ap()
           for L in range(2)]
    kvr = [nc.dram_tensor(f"kv_recv{L}", [2, 2, NGRP, 128, 2048], F16, kind="Internal").ap()
           for L in range(2)]
    g_snd = nc.dram_tensor("g_snd", [128, 3], F32, kind="Internal").ap()
    g_rcv = nc.dram_tensor("g_rcv", [8, 128, 3], F32, kind="Internal").ap()
    RG2 = [[0, 1], [2, 3], [4, 5], [6, 7]]
    RG8 = [[0, 1, 2, 3, 4, 5, 6, 7]]

    es = ExitStack()
    with es:
        sb = lambda name, shape, dt: es.enter_context(nc.sbuf_tensor(name, list(shape), dt))[:, :]
        ps = lambda name: es.enter_context(nc.psum_tensor(name, [128, 512], F32))
        sem = lambda name: Sem(es.enter_context(nc.semaphore(name)))

        xb = sb("xb", (128, DCH * TOK), F32)
        xnh = sb("xnh_s", (128, DCH * TOK), F16)
        xnl = sb("xnl_s", (128, DCH * TOK), F16)
        a32 = sb("a32", (128, 6144), F32)
        a16 = sb("a16", (128, 20480), F16)
        c_bq = sb("c_bq", (128, 16), F32)
        c_bk = sb("c_bk", (128, 4), F32)
        c_bvb = sb("c_bvb", (128, 512), F32)
        c_bo = sb("c_bo", (128, 16), F32)
        c_b0 = sb("c_b0", (128, 64), F32)
        c_b1 = sb("c_b1", (128, 16), F32)
        c_cos = sb("c_cos", (128, TOK), F32)
        c_sin = sb("c_sin", (128, TOK), F32)
        c_kidx = sb("c_kidx", (128, 16), F32)
        qidxb = sb("qidxb_s", (128, TOK), F16)
        ones_r = sb("ones_r", (1, 128), F32)
        ones_c32 = sb("ones_c32", (128, 1), F32)
        ones_c16 = sb("ones_c16", (128, 1), F16)
        g_stg = sb("g_stg", (128, 512), F32)
        g_part = sb("g_part", (128, 72), F32)
        g_all = sb("g_all", (128, 24), F32)
        g_thr = sb("g_thr", (128, 8), F32)
        c_eps = sb("c_eps", (128, 1), F32)
        c_geps = sb("c_geps", (128, 1), F32)

        pb = [ps(f"pb{i}") for i in range(8)]
        pba = [p[:, :] for p in pb]

        s_syn = sem("s_syn")
        s_scl = sem("s_scl")
        s_gpd = sem("s_gpd")
        s_cc = sem("s_cc")
        s_ten = sem("s_ten")
        s_act = sem("s_act")
        s_dve = sem("s_dve")
        s_gps = sem("s_gps")

        E = P.emit

        def dma(eng, out, in_, wait=None, waits=()):
            s = {"syn": s_syn, "act": s_scl, "gps": s_gpd}[eng]
            return E(eng, lambda e, o=out, i=in_: e.dma_start(out=o, in_=i),
                     wait=wait, waits=waits, inc=(s, 16))

        m_x0q = []
        for qq in range(4):
            m_x0q.append(dma("syn", xb[:, qq * 4096:(qq + 1) * 4096],
                             x0t[:, qq * 4096:(qq + 1) * 4096]))
        for dst, src in ((c_bq, bq_h), (c_bk, bk_h), (c_bvb, bvb_h), (c_bo, bo_h),
                         (c_b0, b0_h), (c_b1, b1_h), (c_cos, cos_h), (c_sin, sin_h),
                         (c_kidx, kidx_h), (qidxb, qidx_h)):
            m_init = dma("syn", dst, src)
        E("dve", lambda e: e.memset(ones_r, 1.0), inc=(s_dve, 1))
        E("dve", lambda e: e.memset(ones_c32, 1.0), inc=(s_dve, 1))
        E("dve", lambda e: e.memset(c_eps, 1e-5), inc=(s_dve, 1))
        E("dve", lambda e: e.memset(c_geps, 0.5e-8), inc=(s_dve, 1))
        m_ones = E("dve", lambda e: e.memset(ones_c16, 1.0), inc=(s_dve, 1))

        def emit_ln(tag, entry_d, entry_ws, entry_per_dc=None):
            A = Bump(a32, 6144, f"ln-{tag}")
            sq0 = A.take(1024)
            sq1 = A.take(1024)
            mrowb = A.take(1024)
            vrowb = A.take(1024)
            m_b = A.take(1024)
            r_b = A.take(1024)
            mrow = mrowb[0:1, :]
            vrow = vrowb[0:1, :]

            m_sq = [None, None]
            m_var_mm = [None, None]
            for dc in range(DCH):
                par = dc % 2
                sqb = (sq0, sq1)[par]
                wsx = entry_ws if dc == 0 else ()
                if entry_per_dc is not None and dc % 4 == 0:
                    wsx = wsx + (entry_per_dc[dc // 4],)
                m_sq[par] = E("act",
                              lambda e, o=sqb, i=xb[:, dc * TOK:(dc + 1) * TOK]:
                              e.activation(out=o, in_=i, func=AF.Square),
                              wait=m_var_mm[par] if dc >= 2 else
                              (None if entry_per_dc is not None else entry_d),
                              waits=wsx, inc=(s_act, 1))
                for tc in range(2):
                    E("ten", lambda e, p=pba[tc][0:1, :],
                      r=xb[:, dc * TOK + tc * 512: dc * TOK + tc * 512 + 512],
                      st=(dc == 0), sp=(dc == DCH - 1):
                      e.matmul(p, ones_c32, r, start=st, stop=sp),
                      wait=entry_d if (dc == 0 and tc == 0 and entry_per_dc is None) else None,
                      waits=((entry_ws + ((m_ones,) if m_ones else ())) if (dc == 0 and tc == 0) else ())
                      + (wsx if (entry_per_dc is not None and dc % 4 == 0 and tc == 0) else ()),
                      inc=(s_ten, 1) if dc == DCH - 1 else None)
                for tc in range(2):
                    m_var_mm[par] = E("ten", lambda e, p=pba[2 + tc][0:1, :],
                                      r=sqb[:, tc * 512:tc * 512 + 512],
                                      st=(dc == 0), sp=(dc == DCH - 1):
                                      e.matmul(p, ones_c32, r, start=st, stop=sp),
                                      wait=m_sq[par] if tc == 0 else None,
                                      inc=(s_ten, 1))
            m_sums = s_ten.mark()
            mr = None
            for tc in range(2):
                mr = E("act", lambda e, o=mrow[:, tc * 512:tc * 512 + 512], p=pba[tc][0:1, :]:
                       e.activation(out=o, in_=p, func=AF.Identity, scale=1.0 / D),
                       wait=m_sums if tc == 0 else None, inc=(s_act, 1))
                mr = E("act", lambda e, o=vrow[:, tc * 512:tc * 512 + 512], p=pba[2 + tc][0:1, :]:
                       e.activation(out=o, in_=p, func=AF.Identity, scale=1.0 / D),
                       inc=(s_act, 1))
            m_rows = mr
            mv = E("dve", lambda e: e.tensor_tensor(sq0[0:1, :], mrow, mrow, ALU.mult),
                   wait=m_rows, inc=(s_dve, 1))
            mv = E("dve", lambda e: e.tensor_tensor(vrow, vrow, sq0[0:1, :], ALU.subtract),
                   inc=(s_dve, 1))
            mstd = E("act", lambda e: e.activation(out=sq1[0:1, :], in_=vrow,
                                                   func=AF.Sqrt, bias=c_eps[0:1, :]),
                     wait=mv, inc=(s_act, 1))
            mrcp = E("dve", lambda e: e.reciprocal(out=vrow, in_=sq1[0:1, :]),
                     wait=mstd, inc=(s_dve, 1))
            bms = None
            for tc in range(2):
                b1 = E("ten", lambda e, p=pba[4][:, :], r=mrow[:, tc * 512:tc * 512 + 512]:
                       e.matmul(p, ones_r, r, start=True, stop=True),
                       wait=mrcp if tc == 0 else None,
                       waits=(bms,) if tc == 1 else (), inc=(s_ten, 1))
                bms = E("act", lambda e, o=m_b[:, tc * 512:tc * 512 + 512], p=pba[4][:, :]:
                        e.copy(out=o, in_=p), wait=b1, inc=(s_act, 1))
                b2 = E("ten", lambda e, p=pba[5][:, :], r=vrow[:, tc * 512:tc * 512 + 512]:
                       e.matmul(p, ones_r, r, start=True, stop=True),
                       waits=(bms,) if tc == 1 else (), inc=(s_ten, 1))
                bms = E("act", lambda e, o=r_b[:, tc * 512:tc * 512 + 512], p=pba[5][:, :]:
                        e.copy(out=o, in_=p), wait=b2, inc=(s_act, 1))
            m_bc = bms
            mh = [None, None]
            ml = None
            for dc in range(DCH):
                par = dc % 2
                tb = (sq0, vrowb)[par]
                pbuf = (sq1, mrowb)[par]
                E("dve", lambda e, o=tb, i=xb[:, dc * TOK:(dc + 1) * TOK]:
                  e.tensor_tensor(o, i, m_b, ALU.subtract),
                  wait=m_bc if dc < 2 else None, inc=(s_dve, 1))
                E("dve", lambda e, o=pbuf, i=tb: e.tensor_tensor(o, i, r_b, ALU.mult),
                  wait=mh[par], inc=(s_dve, 1))
                mp = s_dve.mark()
                mh[par] = E("act", lambda e, o=xnh[:, dc * TOK:(dc + 1) * TOK], i=pbuf:
                            e.copy(out=o, in_=i), wait=mp, inc=(s_act, 1))
                ml = E("dve", lambda e, o=xnl[:, dc * TOK:(dc + 1) * TOK], i=pbuf,
                       h2=xnh[:, dc * TOK:(dc + 1) * TOK]:
                       e.tensor_tensor(o, i, h2, ALU.subtract), wait=mh[par], inc=(s_dve, 1))
            return ml, mh[(DCH - 1) % 2], s_ten.mark()

        _PHASE[0] = "ln0"
        m_xn_d, m_xn_a, m_ln_t = emit_ln("L0", None, (m_init,), entry_per_dc=m_x0q)
        if _TAPS and _STAGE in ("ln", "kv", "ffn", "heads"):
            dma("syn", taps["xnh"], xnh, wait=m_xn_d, waits=(m_xn_a,))
            dma("syn", taps["xnl"], xnl, wait=m_xn_d)

        stage_order = ["ln", "kv", "ffn", "heads", "ln1", "bl", "full"]
        sidx = stage_order.index(_STAGE)

        def stage_ge(s):
            return stage_order.index(s) <= sidx

        def emit_kv(L, wk_h4, wv_h4, kv_send, rope, has_kb, has_vb,
                    m_xn_d, m_xn_a, prev_ten):
            A = Bump(a32, 6144, f"kv-{L}")
            A16 = Bump(a16, 20480, f"kv16-{L}")
            wstg = A.take(2048)
            kx32 = A.take(1024)
            shuf = A.take(1024)
            wwh = [A16.take(4096), A16.take(4096)]
            wwl = [A16.take(4096), A16.take(4096)]
            khi = A16.take(1024)
            klo = A16.take(1024)

            def stage(src4, guards_ten, m_stg_dve, first_ws=()):
                m_split = [None, None]
                for hf in range(2):
                    for qq in range(2):
                        idx = hf * 2 + qq
                        md = dma("syn", wstg, src4[idx], wait=m_stg_dve,
                                 waits=first_ws if idx == 0 else ())
                        ma = E("act", lambda e, o=wwh[hf][:, qq * 2048:qq * 2048 + 2048], i=wstg:
                               e.copy(out=o, in_=i), wait=md,
                               waits=(guards_ten[hf],) if (qq == 0 and guards_ten[hf]) else (),
                               inc=(s_act, 1))
                        m_stg_dve = E("dve", lambda e, o=wwl[hf][:, qq * 2048:qq * 2048 + 2048],
                                      i=wstg, h2=wwh[hf][:, qq * 2048:qq * 2048 + 2048]:
                                      e.tensor_tensor(o, i, h2, ALU.subtract),
                                      wait=ma, inc=(s_dve, 1))
                    m_split[hf] = m_stg_dve
                return m_split, m_stg_dve

            m_split, m_stg_dve = stage(wk_h4, (prev_ten, prev_ten), m_xn_d,
                                       first_ws=(m_xn_a,))

            m_khalf0 = None
            m_kmm = [None] * 8
            for hf in range(2):
                for kc in range(4):
                    for tc in range(2):
                        bank = pba[kc * 2 + tc]
                        for j in range(8):
                            dc = hf * 8 + j
                            lw = wwh[hf][:, j * 512 + kc * 128: j * 512 + kc * 128 + 128]
                            ll = wwl[hf][:, j * 512 + kc * 128: j * 512 + kc * 128 + 128]
                            rh = xnh[:, dc * TOK + tc * 512: dc * TOK + tc * 512 + 512]
                            rl = xnl[:, dc * TOK + tc * 512: dc * TOK + tc * 512 + 512]
                            w = m_split[hf] if (kc == 0 and tc == 0 and j == 0) else None
                            st = (hf == 0 and j == 0)
                            sp = (hf == 1 and j == 7)
                            E("ten", lambda e, p=bank, a=lw, b=rh, st=st:
                              e.matmul(p, a, b, start=st, stop=False), wait=w)
                            E("ten", lambda e, p=bank, a=lw, b=rl:
                              e.matmul(p, a, b, start=False, stop=False))
                            last = (hf == 1 and kc == 3 and tc == 1 and j == 7)
                            h0last = (hf == 0 and kc == 3 and tc == 1 and j == 7)
                            mk = E("ten", lambda e, p=bank, a=ll, b=rh, sp=sp:
                                   e.matmul(p, a, b, start=False, stop=sp),
                                   inc=(s_ten, 1) if (sp or h0last) else None)
                            if h0last:
                                m_khalf0 = mk
                            if sp:
                                m_kmm[kc * 2 + tc] = mk
            m_kmm_last = s_ten.mark()

            m_kevac = [None] * 8
            last_send = None
            prev_rope = None
            prev_dve = None
            prev_act = None
            for kc in range(4):
                me = None
                for tc in range(2):
                    ws = ()
                    if tc == 0 and kc > 0:
                        ws = ((s_scl, last_send[1]),)
                        if prev_dve:
                            ws = ws + ((s_dve, prev_dve[1]),)
                    if tc == 0 and kc == 0:
                        ws = ((s_syn, m_init[1]),)
                    osl = kx32[:, tc * 512:tc * 512 + 512]
                    if has_kb:
                        me = E("act", lambda e, o=osl, p=pba[kc * 2 + tc], bi=c_bk[:, kc:kc + 1]:
                               e.activation(out=o, in_=p, func=AF.Identity, bias=bi),
                               wait=m_kmm[kc * 2 + tc], waits=ws, inc=(s_act, 1))
                    else:
                        me = E("act", lambda e, o=osl, p=pba[kc * 2 + tc]:
                               e.copy(out=o, in_=p),
                               wait=m_kmm[kc * 2 + tc], waits=ws, inc=(s_act, 1))
                    m_kevac[kc * 2 + tc] = me
                if rope:
                    d1 = dma("act", shuf[0:64, :], kx32[64:128, :], wait=me,
                             waits=((s_dve, prev_rope[1]),) if prev_rope else ())
                    d2 = dma("act", shuf[64:128, :], kx32[0:64, :])
                    E("dve", lambda e: e.tensor_tensor(shuf, shuf, c_sin, ALU.mult),
                      wait=d2, inc=(s_dve, 1))
                    E("dve", lambda e: e.tensor_tensor(kx32, kx32, c_cos, ALU.mult),
                      inc=(s_dve, 1))
                    mr3 = E("dve", lambda e: e.tensor_tensor(kx32, kx32, shuf, ALU.add),
                            inc=(s_dve, 1))
                    prev_rope = mr3
                    wsrc = mr3
                else:
                    wsrc = None
                mhk = E("act", lambda e: e.copy(out=khi, in_=kx32), wait=wsrc,
                        waits=((s_scl, last_send[1]),) if last_send else (), inc=(s_act, 1))
                mlk = E("dve", lambda e: e.tensor_tensor(klo, kx32, khi, ALU.subtract),
                        wait=mhk, inc=(s_dve, 1))
                prev_dve = mlk
                prev_act = mhk
                dma("act", kv_send[0, kc, :, 0:1024], khi, wait=mhk)
                last_send = dma("act", kv_send[1, kc, :, 0:1024], klo, wait=mlk)

            m_vsplit, m_stg_dve = stage(wv_h4, (m_khalf0, m_kmm_last), m_stg_dve)
            m_vmm = [None] * 8
            for hf in range(2):
                for tt in range(8):
                    bank = pba[tt]
                    for j in range(8):
                        dc = hf * 8 + j
                        lh = xnh[:, dc * TOK + tt * 128: dc * TOK + tt * 128 + 128]
                        llo = xnl[:, dc * TOK + tt * 128: dc * TOK + tt * 128 + 128]
                        rh = wwh[hf][:, j * 512: j * 512 + 512]
                        rl = wwl[hf][:, j * 512: j * 512 + 512]
                        w = m_vsplit[hf] if (tt == 0 and j == 0) else None
                        ws = ((s_act, m_kevac[tt][1]),) if (hf == 0 and j == 0) else ()
                        st = (hf == 0 and j == 0)
                        sp = (hf == 1 and j == 7)
                        E("ten", lambda e, p=bank, a=lh, b=rh, st=st:
                          e.matmul(p, a, b, start=st, stop=False), wait=w, waits=ws)
                        E("ten", lambda e, p=bank, a=lh, b=rl:
                          e.matmul(p, a, b, start=False, stop=False))
                        mk = E("ten", lambda e, p=bank, a=llo, b=rh, sp=sp:
                               e.matmul(p, a, b, start=False, stop=sp),
                               inc=(s_ten, 1) if sp else None)
                        if sp:
                            m_vmm[tt] = mk
            vhi = khi
            vlo = klo
            v32 = kx32[:, 0:512]
            for tt in range(8):
                ws = ()
                if tt == 0:
                    ws = (((s_act, prev_act[1]),) if prev_act else ()) + \
                         (((s_scl, last_send[1]),) if last_send else ())
                if has_vb:
                    mve = E("dve", lambda e, o=v32, p=pba[tt]:
                            e.tensor_tensor(o, p, c_bvb, ALU.add),
                            wait=m_vmm[tt], waits=ws, inc=(s_dve, 1))
                else:
                    mve = E("dve", lambda e, o=v32, p=pba[tt]: e.tensor_copy(out=o, in_=p),
                            wait=m_vmm[tt], waits=ws, inc=(s_dve, 1))
                mvh = E("act", lambda e, o=vhi[:, 0:512], i=v32: e.copy(out=o, in_=i),
                        wait=mve, waits=((s_scl, last_send[1]),), inc=(s_act, 1))
                mvl = E("dve", lambda e, o=vlo[:, 0:512], i=v32, h2=vhi[:, 0:512]:
                        e.tensor_tensor(o, i, h2, ALU.subtract), wait=mvh, inc=(s_dve, 1))
                for g in range(4):
                    dma("act", kv_send[0, g, :, 1024 + tt * 128: 1024 + (tt + 1) * 128],
                        vhi[:, g * 128:g * 128 + 128], wait=mvh if g == 0 else None)
                for g in range(4):
                    last_send = dma("act", kv_send[1, g, :, 1024 + tt * 128: 1024 + (tt + 1) * 128],
                                    vlo[:, g * 128:g * 128 + 128], wait=mvl if g == 0 else None)
            mcc = E("gps", lambda e: e.collective_compute(
                "AllGather", ALU.bypass, replica_groups=RG2,
                ins=[kv_send[:, :, :, :]], outs=[kvr[L][:, :, :, :]]),
                wait=last_send, inc=(s_cc, 1))
            return dict(cc=mcc, ten=s_ten.mark(), act=s_act.mark(),
                        dve=s_dve.mark(), scl=last_send)

        kv0 = None
        if stage_ge("kv"):
            _PHASE[0] = "kv0"
            kv0 = emit_kv(0, [wk0_h[i] for i in range(4)], [wv0_h[i] for i in range(4)],
                          kvs[0], True, True, True, m_xn_d, m_xn_a, None)
            if _TAPS:
                dma("syn", taps["kv"], kvr[0][:, :, :, :, :], wait=None,
                    waits=((s_cc, kv0["cc"][1]),))

        gam = {"dma": None, "red": None, "pc": 0}

        def gamma_pump(nmax):
            while gam["pc"] < 66 and nmax > 0:
                pc = gam["pc"]
                gam["dma"] = dma("gps", g_stg, gsrc_h[pc], wait=gam["red"])
                gam["red"] = E("dve", lambda e, o=g_part[:, pc:pc + 1]:
                               e.tensor_reduce(o, g_stg, mybir.AxisListType.X,
                                               ALU.add, apply_absolute_value=True),
                               wait=gam["dma"], inc=(s_dve, 1))
                gam["pc"] += 1
                nmax -= 1

        def emit_ffn0(prev):
            A = Bump(a32, 6144, "ffn")
            A16 = Bump(a16, 20480, "ffn16")
            stg = A.take(4096)
            wbh = A16.take(8192)
            wbl = A16.take(8192)
            d16 = A16.take(4096)

            def half(buf, hf):
                return buf[:, hf * 4096:hf * 4096 + 4096]

            st8 = {"dve": prev["dve"]}

            def stage_half(src_ap, hf, gate_ten, extra_ws=()):
                md = dma("syn", stg, src_ap, wait=st8["dve"], waits=extra_ws)
                ma = E("act", lambda e, o=half(wbh, hf), i=stg:
                       e.copy(out=o, in_=i), wait=md,
                       waits=((s_ten, gate_ten[1]),), inc=(s_act, 1))
                ms = E("dve", lambda e, o=half(wbl, hf), i=stg, h2=half(wbh, hf):
                       e.tensor_tensor(o, i, h2, ALU.subtract),
                       wait=ma, inc=(s_dve, 1))
                st8["dve"] = ms
                return ms

            mE0_prev = prev["ten"]
            mE1_prev = prev["ten"]
            m_evac_last = prev["dve"]
            m_evac = [prev["dve"]] * 8
            first_gelu = True
            for hc in range(_FFNC):
                if stage_ge("bl") and hc >= 3:
                    gamma_pump(6)
                ws0 = ()
                if hc == 0:
                    ws0 = ((s_act, prev["act"][1]), (s_scl, prev["scl"][1]))
                m_sp_w0 = [None, None]
                m_sp_w0[0] = stage_half(w00_h[hc, 0], 0, mE0_prev, ws0)
                m_sp_w0[1] = stage_half(w00_h[hc, 1], 1, mE1_prev)
                m_dmm = [None] * 8
                mD0 = None
                for hf in range(2):
                    for hc2 in range(4):
                        for tc in range(2):
                            bank = pba[hc2 * 2 + tc]
                            for j in range(8):
                                dc = hf * 8 + j
                                lwh = wbh[:, dc * 512 + hc2 * 128: dc * 512 + hc2 * 128 + 128]
                                lwl = wbl[:, dc * 512 + hc2 * 128: dc * 512 + hc2 * 128 + 128]
                                rh = xnh[:, dc * TOK + tc * 512: dc * TOK + tc * 512 + 512]
                                w = m_sp_w0[hf] if (hc2 == 0 and tc == 0 and j == 0) else None
                                ws = ()
                                if hf == 0 and hc2 == 0 and tc == 0 and j == 0:
                                    ws = ((s_dve, m_evac_last[1]),)
                                st = (hf == 0 and j == 0)
                                sp = (hf == 1 and j == 7)
                                h0last = (hf == 0 and hc2 == 3 and tc == 1 and j == 7)
                                E("ten", lambda e, p=bank, a=lwh, b=rh, st=st:
                                  e.matmul(p, a, b, start=st, stop=False), wait=w, waits=ws)
                                mk = E("ten", lambda e, p=bank, a=lwl, b=rh, sp=sp:
                                       e.matmul(p, a, b, start=False, stop=sp),
                                       inc=(s_ten, 1) if (sp or h0last) else None)
                                if h0last:
                                    mD0 = mk
                                if sp:
                                    m_dmm[hc2 * 2 + tc] = mk
                    if hf == 0:
                        m_sp_w1h0 = stage_half(w10_h[hc, 0], 0, mD0)
                m_dmm_last = s_ten.mark()
                m_gel = None
                for hc2 in range(4):
                    for tc in range(2):
                        ws = ()
                        if first_gelu:
                            ws = ((s_scl, prev["scl"][1]),)
                            first_gelu = False
                        m_gel = E("act", lambda e,
                                  o=d16[:, hc2 * 1024 + tc * 512: hc2 * 1024 + tc * 512 + 512],
                                  p=pba[hc2 * 2 + tc], bi=c_b0[:, hc * 4 + hc2: hc * 4 + hc2 + 1]:
                                  e.activation(out=o, in_=p, func=AF.Gelu, bias=bi),
                                  wait=m_dmm[hc2 * 2 + tc], waits=ws, inc=(s_act, 1))
                if _TAPS and hc == 0:
                    dma("syn", taps["d16"], d16, wait=m_gel)
                m_sp_w1h1 = stage_half(w10_h[hc, 1], 1, m_dmm_last)
                mE0 = None
                for og in range(4):
                    for pair in range(2):
                        for oc in range(og * 4, og * 4 + 4):
                            for tc in range(2):
                                bank = pba[(oc % 4) * 2 + tc]
                                for pc in (pair * 2, pair * 2 + 1):
                                    lwh = wbh[:, pc * 2048 + oc * 128: pc * 2048 + oc * 128 + 128]
                                    lwl = wbl[:, pc * 2048 + oc * 128: pc * 2048 + oc * 128 + 128]
                                    rh = d16[:, pc * 1024 + tc * 512: pc * 1024 + tc * 512 + 512]
                                    w = None
                                    ws = ()
                                    first_of_grp = (pc == pair * 2)
                                    if first_of_grp and pair == 0:
                                        ws = ((s_dve, m_evac[(oc % 4) * 2 + tc][1]),)
                                        if og == 0 and oc == 0 and tc == 0:
                                            w = m_sp_w1h0
                                            ws = ws + ((s_act, m_gel[1]),)
                                    if first_of_grp and pair == 1 and og == 0 and oc == 4 * og and tc == 0:
                                        w = m_sp_w1h1
                                    st = (pair == 0 and pc == 0)
                                    sp = (pair == 1 and pc == 3)
                                    p0last = (pair == 0 and og == 3 and oc == 15
                                              and tc == 1 and pc == 1)
                                    E("ten", lambda e, p=bank, a=lwh, b=rh, st=st:
                                      e.matmul(p, a, b, start=st, stop=False),
                                      wait=w, waits=ws)
                                    mk2 = E("ten", lambda e, p=bank, a=lwl, b=rh, sp=sp:
                                            e.matmul(p, a, b, start=False, stop=sp),
                                            inc=(s_ten, 1) if (sp or p0last) else None)
                                    if p0last:
                                        mE0 = mk2
                                if pair == 1:
                                    mmk = s_ten.mark()
                                    xsl = xb[:, oc * TOK + tc * 512: oc * TOK + tc * 512 + 512]
                                    if hc == 0:
                                        mev = E("dve", lambda e, o=xsl, p=bank,
                                                bi=c_b1[:, oc:oc + 1]:
                                                e.scalar_tensor_tensor(o, p, bi, o,
                                                                       ALU.add, ALU.add),
                                                wait=mmk, inc=(s_dve, 1))
                                    else:
                                        mev = E("dve", lambda e, o=xsl, p=bank:
                                                e.tensor_tensor(o, o, p, ALU.add),
                                                wait=mmk, inc=(s_dve, 1))
                                    m_evac[(oc % 4) * 2 + tc] = mev
                                    m_evac_last = mev
                mE0_prev = mE0 if mE0 is not None else s_ten.mark()
                mE1_prev = s_ten.mark()
            return dict(ten=s_ten.mark(), act=s_act.mark(), dve=m_evac_last,
                        scl=s_scl.mark())

        ffn0 = None
        if stage_ge("ffn"):
            _PHASE[0] = "ffn0"
            ffn0 = emit_ffn0(kv0)

        def emit_heads(L, wq_h, wo_h, kvrL, rope, has_qb, has_ob, mcc, prev):
            A = Bump(a32, 6144, f"hd-{L}")
            A16 = Bump(a16, 20480, f"hd16-{L}")
            stg = A.take(2048)
            q32 = A.take(1024)
            shuf = A.take(1024)
            recipb = [A.take(512), A.take(512)]
            tmp32 = A.take(512)
            exp32b = A.take(512)
            exp32s = [q32[:, 0:512], exp32b]
            msk = shuf[:, 0:512]
            wh = A16.take(2048)
            wl = A16.take(2048)
            qhi = A16.take(1024)
            qlo = A16.take(1024)
            yhi = A16.take(1024)
            ylo = A16.take(1024)
            ehi = [A16.take(512) for _ in range(4)]
            elo = [A16.take(512) for _ in range(4)]
            kvbh = [A16.take(2048), A16.take(2048)]
            kvbl = [A16.take(2048), A16.take(2048)]
            recip = tmp32[0:1, :]

            m_xadd = [prev["dve"]] * 8
            m_xadd_last = prev["dve"]
            m_wo_ten = prev["ten"]
            m_kvfree = [prev["ten"], prev["ten"]]
            m_stg_dve = prev["dve"]
            prev_rope = None
            prev_ropedma = None
            for g in range(NGRP):
                m_kvload = [None, None]
                for hh in range(4):
                    h = g * 4 + hh
                    ws = ()
                    if h == 0:
                        ws = ((s_act, prev["act"][1]), (s_scl, prev["scl"][1]),
                              (s_ten, prev["ten"][1]))
                    mdq = dma("syn", stg, wq_h[h], wait=m_stg_dve, waits=ws)
                    maq = E("act", lambda e: e.copy(out=wh, in_=stg), wait=mdq,
                            waits=((s_ten, m_wo_ten[1]),), inc=(s_act, 1))
                    msq = E("dve", lambda e: e.tensor_tensor(wl, stg, wh, ALU.subtract),
                            wait=maq, inc=(s_dve, 1))
                    if hh == 0:
                        for r in range(2):
                            wsl = ((s_cc, mcc[1]),) if (h == 0 and r == 0) else ()
                            dma("act", kvbh[r], kvrL[r, 0, g, :, :],
                                wait=(s_ten, m_kvfree[r][1]), waits=wsl)
                            dma("act", kvbl[r], kvrL[r, 1, g, :, :])
                            m_kvload[r] = s_scl.mark()
                    mq = [None, None]
                    for tc in range(2):
                        bank = pba[tc]
                        for dc in range(DCH):
                            lwh = wh[:, dc * 128: dc * 128 + 128]
                            lwl = wl[:, dc * 128: dc * 128 + 128]
                            rh = xnh[:, dc * TOK + tc * 512: dc * TOK + tc * 512 + 512]
                            rl = xnl[:, dc * TOK + tc * 512: dc * TOK + tc * 512 + 512]
                            w = msq if dc == 0 else None
                            ws = ((s_dve, m_xadd[tc][1]),) if dc == 0 else ()
                            E("ten", lambda e, p=bank, a=lwh, b=rh, st=(dc == 0):
                              e.matmul(p, a, b, start=st, stop=False), wait=w, waits=ws)
                            E("ten", lambda e, p=bank, a=lwh, b=rl:
                              e.matmul(p, a, b, start=False, stop=False))
                            mq[tc] = E("ten", lambda e, p=bank, a=lwl, b=rh, sp=(dc == DCH - 1):
                                       e.matmul(p, a, b, start=False, stop=sp),
                                       inc=(s_ten, 1) if dc == DCH - 1 else None)
                    meq = None
                    for tc in range(2):
                        ws = ()
                        if tc == 0 and rope and prev_ropedma is not None:
                            ws = ((s_scl, prev_ropedma[1]),)
                        osl = q32[:, tc * 512:tc * 512 + 512]
                        if has_qb:
                            meq = E("act", lambda e, o=osl, p=pba[tc], bi=c_bq[:, h:h + 1]:
                                    e.activation(out=o, in_=p, func=AF.Identity, bias=bi),
                                    wait=mq[tc], waits=ws, inc=(s_act, 1))
                        else:
                            meq = E("act", lambda e, o=osl, p=pba[tc]: e.copy(out=o, in_=p),
                                    wait=mq[tc], waits=ws, inc=(s_act, 1))
                    if rope:
                        d1 = dma("act", shuf[0:64, :], q32[64:128, :], wait=meq,
                                 waits=((s_dve, prev_rope[1]),) if prev_rope else ())
                        d2 = dma("act", shuf[64:128, :], q32[0:64, :])
                        prev_ropedma = d2
                        E("dve", lambda e: e.tensor_tensor(shuf, shuf, c_sin, ALU.mult),
                          wait=d2, inc=(s_dve, 1))
                        E("dve", lambda e: e.tensor_tensor(q32, q32, c_cos, ALU.mult),
                          inc=(s_dve, 1))
                        mr3 = E("dve", lambda e: e.tensor_tensor(q32, q32, shuf, ALU.add),
                                inc=(s_dve, 1))
                        prev_rope = mr3
                        mqh = E("act", lambda e: e.copy(out=qhi, in_=q32), wait=mr3,
                                inc=(s_act, 1))
                    else:
                        mqh = E("act", lambda e: e.copy(out=qhi, in_=q32), inc=(s_act, 1))
                    mql = E("dve", lambda e: e.tensor_tensor(qlo, q32, qhi, ALU.subtract),
                            wait=mqh, inc=(s_dve, 1))
                    mdo = dma("syn", stg, wo_h[h], wait=msq)
                    mao = E("act", lambda e: e.copy(out=wh, in_=stg), wait=mdo,
                            waits=((s_ten, mq[1][1]),), inc=(s_act, 1))
                    mso = E("dve", lambda e: e.tensor_tensor(wl, stg, wh, ALU.subtract),
                            wait=mao, inc=(s_dve, 1))
                    m_stg_dve = mso
                    tiles = []
                    for r in range(2):
                        for tc in range(2):
                            nk = NKTR[tc][r]
                            for j in range(nk):
                                tiles.append((r, tc, j, r == 1 and j == nk - 1))
                    nt = len(tiles)
                    stop0 = next(t for t, q in enumerate(tiles)
                                 if q[1] == 0 and q[3])
                    lastr0 = max(t for t, q in enumerate(tiles) if q[0] == 0)
                    m_e_act = [None] * 6
                    m_el = [None] * 6
                    m_tile = [None] * 6
                    m_summ = [None, None]
                    m_avmm = [None, None]
                    sum_started = [False, False]
                    seen_r = [False, False]

                    SB = (2, 3, 0, 1)
                    ehi6 = ehi + [yhi[:, 0:512], yhi[:, 512:1024]]
                    elo6 = elo + [ylo[:, 0:512], ylo[:, 512:1024]]

                    def emit_scores(t):
                        r, tc, j, _stop = tiles[t]
                        kt = r * 8 + j
                        sp_ = t % 6
                        exp32 = exp32s[t % 2]
                        sbank = pba[SB[t % 4]]
                        lkh = kvbh[r][:, j * 128:(j + 1) * 128]
                        lkl = kvbl[r][:, j * 128:(j + 1) * 128]
                        rqh = qhi[:, tc * 512:tc * 512 + 512]
                        rql = qlo[:, tc * 512:tc * 512 + 512]
                        ws = ()
                        if not seen_r[r]:
                            ws = ((s_scl, m_kvload[r][1]),)
                            seen_r[r] = True
                        if t >= 4 and m_el[(t - 4) % 6] is not None:
                            ws = ws + ((s_dve, m_el[(t - 4) % 6][1]),)

                        E("ten", lambda e, p=sbank, a=lkh, b=rqh:
                          e.matmul(p, a, b, start=True, stop=False),
                          wait=mql, waits=ws)
                        E("ten", lambda e, p=sbank, a=lkh, b=rql:
                          e.matmul(p, a, b, start=False, stop=False))
                        msc = E("ten", lambda e, p=sbank, a=lkl, b=rqh:
                                e.matmul(p, a, b, start=False, stop=True),
                                inc=(s_ten, 1))
                        mgw = (s_syn, m_init[1]) if t == 0 else None
                        E("dve", lambda e, o=msk,
                          qv=qidxb[:, tc * 512:tc * 512 + 512],
                          kvx=c_kidx[:, kt:kt + 1]:
                          e.tensor_scalar(o, qv, kvx, NEG, ALU.is_lt, ALU.mult),
                          wait=mgw, inc=(s_dve, 1))
                        gex = m_e_act[(t - 2) % 6] if t >= 2 else None
                        mev = E("dve", lambda e, p=sbank, x=exp32:
                                e.scalar_tensor_tensor(x, p, SCALE, msk,
                                                       ALU.mult, ALU.add),
                                wait=msc,
                                waits=((s_act, gex[1]),) if gex else (),
                                inc=(s_dve, 1))
                        E("act", lambda e, x=exp32: e.activation(out=x, in_=x,
                                                                 func=AF.Exp),
                          wait=mev, inc=(s_act, 1))
                        mhc = E("act", lambda e, o=ehi6[sp_], x=exp32: e.copy(out=o, in_=x),
                                waits=((s_ten, m_tile[sp_][1]),) if m_tile[sp_] else (),
                                inc=(s_act, 1))
                        m_e_act[sp_] = mhc
                        mlc = E("dve", lambda e, o=elo6[sp_], h2=ehi6[sp_], x=exp32:
                                e.tensor_tensor(o, x, h2, ALU.subtract),
                                wait=mhc, inc=(s_dve, 1))
                        m_el[sp_] = mlc

                    def emit_sumav(t):
                        r, tc, j, sp_stop = tiles[t]
                        sp_ = t % 6
                        sum_bank = pba[6 + tc][0:1, :]
                        st = not sum_started[tc]
                        sum_started[tc] = True
                        E("ten", lambda e, p=sum_bank, a=ehi6[sp_], st=st:
                          e.matmul(p, ones_c16, a, start=st, stop=False),
                          wait=m_el[sp_])
                        mk = E("ten", lambda e, p=sum_bank, a=elo6[sp_], sp=sp_stop:
                               e.matmul(p, ones_c16, a, start=False, stop=sp),
                               inc=(s_ten, 1) if sp_stop else None)
                        if sp_stop:
                            m_summ[tc] = mk
                        av_bank = pba[4 + tc]
                        lvh = kvbh[r][:, 1024 + j * 128: 1024 + (j + 1) * 128]
                        lvl = kvbl[r][:, 1024 + j * 128: 1024 + (j + 1) * 128]
                        E("ten", lambda e, p=av_bank, a=lvh, b=ehi6[sp_], st=st:
                          e.matmul(p, a, b, start=st, stop=False))
                        E("ten", lambda e, p=av_bank, a=lvh, b=elo6[sp_]:
                          e.matmul(p, a, b, start=False, stop=False))
                        mk = E("ten", lambda e, p=av_bank, a=lvl, b=ehi6[sp_],
                               sp=sp_stop:
                               e.matmul(p, a, b, start=False, stop=sp),
                               inc=(s_ten, 1))
                        m_tile[sp_] = mk
                        if sp_stop:
                            m_avmm[tc] = mk
                        if hh == 3 and t == lastr0:
                            m_kvfree[0] = s_ten.mark()

                    m_rc = [None, None]
                    m_be = [None, None]

                    def emit_recip(tc):
                        m_rc[tc] = E("dve", lambda e, p=pba[6 + tc][0:1, :]:
                                     e.reciprocal(out=recip, in_=p),
                                     wait=m_summ[tc], inc=(s_dve, 1))

                    def emit_bcast(tc):
                        mbb = E("ten", lambda e, p=pba[6 + tc]:
                                e.matmul(p, ones_r, recip, start=True, stop=True),
                                wait=m_rc[tc], inc=(s_ten, 1))
                        m_be[tc] = E("act", lambda e, o=recipb[tc], p=pba[6 + tc]:
                                     e.activation(out=o, in_=p, func=AF.Identity,
                                                  scale=16.0),
                                     wait=mbb, inc=(s_act, 1))

                    DEPTH = 5
                    for t in range(min(DEPTH, nt)):
                        emit_scores(t)
                    norm0_done = False
                    for t in range(DEPTH, nt + DEPTH):
                        if t < nt:
                            emit_scores(t)
                        k = t - DEPTH
                        emit_sumav(k)
                        if k >= stop0 + 2 and not norm0_done:
                            emit_recip(0)
                            emit_bcast(0)
                            norm0_done = True
                    if not norm0_done:
                        emit_recip(0)
                        emit_bcast(0)
                    emit_recip(1)
                    if hh == 3:
                        m_kvfree[1] = s_ten.mark()
                    myl_t = [None, None]
                    strips_done = s_ten.mark()
                    for tc in range(2):
                        myh_t = E("act", lambda e, o=yhi[:, tc * 512:tc * 512 + 512],
                                  p=pba[4 + tc]:
                                  e.activation(out=o, in_=p, func=AF.Identity,
                                               scale=0.0625),
                                  wait=m_avmm[tc],
                                  waits=((s_ten, strips_done[1]),) if tc == 0 else (),
                                  inc=(s_act, 1))
                        myl_t[tc] = E("dve", lambda e, o=ylo[:, tc * 512:tc * 512 + 512],
                                      p=pba[4 + tc], h2=yhi[:, tc * 512:tc * 512 + 512]:
                                      e.scalar_tensor_tensor(o, p, 0.0625, h2,
                                                             ALU.mult, ALU.subtract),
                                      wait=myh_t, inc=(s_dve, 1))
                    myl = myl_t[1]
                    emit_bcast(1)
                    for oc in range(16):
                        for tc in range(2):
                            idx = oc * 2 + tc
                            bank = pba[idx % 8]
                            lwh = wh[:, oc * 128: oc * 128 + 128]
                            lwl = wl[:, oc * 128: oc * 128 + 128]
                            ryh = yhi[:, tc * 512:tc * 512 + 512]
                            ryl = ylo[:, tc * 512:tc * 512 + 512]
                            w = myl if idx == 0 else None
                            ws = ((s_dve, m_xadd[idx % 8][1]),) if idx >= 8 else ()
                            E("ten", lambda e, p=bank, a=lwh, b=ryh:
                              e.matmul(p, a, b, start=True, stop=False), wait=w, waits=ws)
                            E("ten", lambda e, p=bank, a=lwh, b=ryl:
                              e.matmul(p, a, b, start=False, stop=False))
                            mwo = E("ten", lambda e, p=bank, a=lwl, b=ryh:
                                    e.matmul(p, a, b, start=False, stop=True),
                                    inc=(s_ten, 1))
                            xsl = xb[:, oc * TOK + tc * 512: oc * TOK + tc * 512 + 512]
                            E("dve", lambda e, p=bank, rb=recipb[tc]:
                              e.tensor_tensor(tmp32, p, rb, ALU.mult),
                              wait=mwo, waits=((m_be[tc],) if idx < 2 else ()),
                              inc=(s_dve, 1))
                            if has_ob and h == 0:
                                mxa = E("dve", lambda e, o=xsl, bi=c_bo[:, oc:oc + 1]:
                                        e.scalar_tensor_tensor(o, tmp32, bi, o,
                                                               ALU.add, ALU.add),
                                        inc=(s_dve, 1))
                            else:
                                mxa = E("dve", lambda e, o=xsl:
                                        e.tensor_tensor(o, o, tmp32, ALU.add),
                                        inc=(s_dve, 1))
                            m_xadd[idx % 8] = mxa
                            m_xadd_last = mxa
                    m_wo_ten = s_ten.mark()
            return dict(ten=s_ten.mark(), act=s_act.mark(), dve=m_xadd_last,
                        scl=s_scl.mark())

        m_gamma_tot = None
        if stage_ge("bl"):
            gamma_pump(66)
            mred = gam["red"]
            for wi in range(3):
                mred = E("dve", lambda e, o=g_part[:, 66 + wi:67 + wi],
                         i=g_part[:, wi * 22:(wi + 1) * 22]:
                         e.tensor_reduce(o, i, mybir.AxisListType.X, ALU.add),
                         inc=(s_dve, 1))
            mgs = dma("syn", g_snd, g_part[:, 66:69], wait=mred)
            mgcc = E("gps", lambda e: e.collective_compute(
                "AllGather", ALU.bypass, replica_groups=RG8,
                ins=[g_snd[:, :]], outs=[g_rcv[:, :, :]]),
                wait=mgs, inc=(s_cc, 1))

        l0_done = ffn0
        if stage_ge("heads"):
            _PHASE[0] = "heads0"
            l0_done = emit_heads(0, [wq0_h[i] for i in range(NQH)],
                                 [wo0_h[i] for i in range(NQH)],
                                 kvr[0], True, True, True, kv0["cc"], ffn0)

        if stage_ge("ln1"):
            _PHASE[0] = "ln1"
            m_xn1_d, m_xn1_a, m_ln1_t = emit_ln("L1", l0_done["dve"], ())
            if _TAPS:
                dma("syn", taps["xnh"], xnh, wait=m_xn1_d, waits=(m_xn1_a,))
                dma("syn", taps["xnl"], xnl, wait=m_xn1_d)
        if stage_ge("bl"):
            for r in range(8):
                mgl = dma("syn", g_all[:, r * 3:r * 3 + 3], g_rcv[r, :, :],
                          waits=((s_cc, mgcc[1]),) if r == 0 else ())
            for i in range(4):
                mga = E("dve", lambda e, o=g_part[:, i * 3:i * 3 + 3],
                        a=g_all[:, 2 * i * 3:2 * i * 3 + 3],
                        b=g_all[:, (2 * i + 1) * 3:(2 * i + 1) * 3 + 3]:
                        e.tensor_tensor(o, a, b, ALU.add),
                        wait=mgl if i == 0 else None, inc=(s_dve, 1))
            for i in range(2):
                mga = E("dve", lambda e, o=g_part[:, 12 + i * 3:12 + i * 3 + 3],
                        a=g_part[:, 2 * i * 3:2 * i * 3 + 3],
                        b=g_part[:, (2 * i + 1) * 3:(2 * i + 1) * 3 + 3]:
                        e.tensor_tensor(o, a, b, ALU.add),
                        wait=mga if i == 0 else None, inc=(s_dve, 1))
            mga = E("dve", lambda e:
                    e.tensor_tensor(g_part[:, 69:72], g_part[:, 12:15],
                                    g_part[:, 15:18], ALU.add),
                    wait=mga, inc=(s_dve, 1))
            m_gamma_tot = mga
            thr = []
            n_el = HID1 * D
            for wi in range(3):
                mt1 = E("ten", lambda e, p=pba[2][0:1, 0:1], c=g_part[:, 69 + wi:70 + wi]:
                        e.matmul(p, c, ones_c32, start=True, stop=True),
                        wait=m_gamma_tot,
                        waits=((s_dve, l0_done["dve"][1]),
                               (s_act, l0_done["act"][1])) if wi == 0 else (),
                        inc=(s_ten, 1))
                mt2 = E("act", lambda e, p=pba[2][0:1, 0:1],
                        o=g_thr[0:1, wi:wi + 1]:
                        e.activation(out=o, in_=p, func=AF.Identity,
                                     scale=0.5 / n_el, bias=c_geps[0:1, :]),
                        wait=mt1, inc=(s_act, 1))
                mt3 = E("ten", lambda e, p=pba[3][:, 0:1], r=g_thr[0:1, wi:wi + 1]:
                        e.matmul(p, ones_r, r, start=True, stop=True),
                        wait=mt2, inc=(s_ten, 1))
                mt4 = E("act", lambda e, o=g_thr[:, wi:wi + 1], p=pba[3][:, 0:1]:
                        e.copy(out=o, in_=p), wait=mt3, inc=(s_act, 1))
                mt5 = E("act", lambda e, o=g_thr[:, 3 + wi:4 + wi], i=g_thr[:, wi:wi + 1]:
                        e.activation(out=o, in_=i, func=AF.Identity, scale=-1.0),
                        wait=mt4, inc=(s_act, 1))
                thr.append((g_thr[:, wi:wi + 1], g_thr[:, 3 + wi:4 + wi], mt5))
            if _TAPS:
                dma("syn", taps["thr"], g_thr, wait=None, waits=(thr[2][2],))
                dma("syn", taps["gp"], g_part, wait=m_gamma_tot)
                dma("syn", taps["ga"], g_all, wait=m_gamma_tot)

            _PHASE[0] = "kv1"
            kv1 = emit_kv(1, [wk1_h[i] for i in range(4)], [wv1_h[i] for i in range(4)],
                          kvs[1], False, False, False, m_xn1_d, m_xn1_a,
                          l0_done["ten"])

            def emit_bitlin(prev):
                A = Bump(a32, 6144, "bl")
                A16 = Bump(a16, 20480, "bl16")
                stg = A.take(2048)
                sig = A.take(2048)
                h32 = A.take(2048)
                cmp16 = A16.take(2048)
                q1c = A16.take(4096)
                q2c = A16.take(4096)
                q3c = A16.take(4096)
                hhi = A16.take(2048)
                hlo = A16.take(2048)

                state = {"stg_dve": prev["dve"]}

                def quantize(dst, src_h, hc, ti, dst_free_ten, extra_ws=()):
                    tpos, tneg, mthr = thr[ti]
                    mq = None
                    for hf in range(2):
                        ws = extra_ws if hf == 0 else ()
                        md = dma("syn", stg, src_h[hc, hf], wait=state["stg_dve"], waits=ws)
                        wsA = ()
                        if hf == 0:
                            wsA = ((s_ten, dst_free_ten[1]), mthr)
                        mA = E("dve", lambda e, o=cmp16, i=stg, t=tneg:
                               e.tensor_scalar(o, i, t, 1.0, ALU.is_lt, ALU.mult),
                               wait=md, waits=wsA, inc=(s_dve, 1))
                        mq = E("dve", lambda e, o=dst[:, hf * 2048:hf * 2048 + 2048],
                               i=stg, t=tpos:
                               e.scalar_tensor_tensor(o, i, t, cmp16, ALU.is_gt, ALU.subtract),
                               wait=mA, inc=(s_dve, 1))
                        state["stg_dve"] = mq
                    return mq

                m_f1 = [prev["dve"]] * 8
                m_f1_last = prev["dve"]
                m_hl_prev = None
                m_q3_used = prev["ten"]
                msig_prev = prev["act"]
                first = True
                pre_q = {}
                for hc in range(NBC):
                    ws0 = ((s_act, prev["act"][1]),) if first else ()
                    if hc == 0:
                        mq1 = quantize(q1c, w1q_h, 0, 0, prev["ten"], ws0)
                        mq2 = quantize(q2c, w2q_h, 0, 1, prev["ten"])
                    else:
                        mq1, mq2 = pre_q[hc]
                    mq3 = quantize(q3c, w3q_h, hc, 2, m_q3_used)
                    mm1 = [None] * 4
                    for hc2 in range(2):
                        for tc in range(2):
                            bank = pba[hc2 * 2 + tc]
                            for dc in range(DCH):
                                la = q1c[:, dc * 256 + hc2 * 128: dc * 256 + hc2 * 128 + 128]
                                rh = xnh[:, dc * TOK + tc * 512: dc * TOK + tc * 512 + 512]
                                rl = xnl[:, dc * TOK + tc * 512: dc * TOK + tc * 512 + 512]
                                w = mq1 if dc == 0 else None
                                ws = ()
                                if hc2 == 0 and tc == 0 and dc == 0:
                                    ws = ((s_act, msig_prev[1]),
                                          (s_dve, m_f1_last[1]))
                                E("ten", lambda e, p=bank, a=la, b=rh, st=(dc == 0):
                                  e.matmul(p, a, b, start=st, stop=False),
                                  wait=w, waits=ws)
                                mm1[hc2 * 2 + tc] = E(
                                    "ten", lambda e, p=bank, a=la, b=rl, sp=(dc == DCH - 1):
                                    e.matmul(p, a, b, start=False, stop=sp),
                                    inc=(s_ten, 1) if dc == DCH - 1 else None)
                    msig = [None] * 4
                    for hc2 in range(2):
                        for tc in range(2):
                            ws = ()
                            if hc2 == 0 and tc == 0 and m_hl_prev:
                                ws = ((s_dve, m_hl_prev[1]),)
                            msig[hc2 * 2 + tc] = E(
                                "act", lambda e,
                                o=sig[:, hc2 * 1024 + tc * 512: hc2 * 1024 + tc * 512 + 512],
                                p=pba[hc2 * 2 + tc]:
                                e.activation(out=o, in_=p, func=AF.Sigmoid),
                                wait=mm1[hc2 * 2 + tc],
                                waits=ws + (((s_scl, kv1["scl"][1]),) if first else ()),
                                inc=(s_act, 1))
                    msig_prev = msig[3]
                    mm2 = [None] * 4
                    for hc2 in range(2):
                        for tc in range(2):
                            bank = pba[4 + hc2 * 2 + tc]
                            for dc in range(DCH):
                                la = q2c[:, dc * 256 + hc2 * 128: dc * 256 + hc2 * 128 + 128]
                                rh = xnh[:, dc * TOK + tc * 512: dc * TOK + tc * 512 + 512]
                                rl = xnl[:, dc * TOK + tc * 512: dc * TOK + tc * 512 + 512]
                                w = mq2 if dc == 0 else None
                                E("ten", lambda e, p=bank, a=la, b=rh, st=(dc == 0):
                                  e.matmul(p, a, b, start=st, stop=False), wait=w)
                                mm2[hc2 * 2 + tc] = E(
                                    "ten", lambda e, p=bank, a=la, b=rl, sp=(dc == DCH - 1):
                                    e.matmul(p, a, b, start=False, stop=sp),
                                    inc=(s_ten, 1) if dc == DCH - 1 else None)
                    m_hl = None
                    for hc2 in range(2):
                        for tc in range(2):
                            sl = slice(hc2 * 1024 + tc * 512, hc2 * 1024 + tc * 512 + 512)
                            mh32 = E("dve", lambda e, o=h32[:, sl], s2=sig[:, sl],
                                     p=pba[4 + hc2 * 2 + tc]:
                                     e.tensor_tensor(o, s2, p, ALU.mult),
                                     wait=mm2[hc2 * 2 + tc],
                                     waits=(msig[hc2 * 2 + tc],), inc=(s_dve, 1))
                            mhh = E("act", lambda e, o=hhi[:, sl], i=h32[:, sl]:
                                    e.copy(out=o, in_=i), wait=mh32,
                                    waits=(((s_scl, kv1["scl"][1]),) if first else ()),
                                    inc=(s_act, 1))
                            m_hl = E("dve", lambda e, o=hlo[:, sl], i=h32[:, sl],
                                     h2=hhi[:, sl]:
                                     e.tensor_tensor(o, i, h2, ALU.subtract),
                                     wait=mhh, inc=(s_dve, 1))
                    m_hl_prev = m_hl
                    first = False
                    for oc in range(16):
                        for tc in range(2):
                            idx = oc * 2 + tc
                            bank = pba[idx % 8]
                            cnt = 0
                            for pc in range(2):
                                la = q3c[:, pc * 2048 + oc * 128: pc * 2048 + oc * 128 + 128]
                                rh = hhi[:, pc * 1024 + tc * 512: pc * 1024 + tc * 512 + 512]
                                rl = hlo[:, pc * 1024 + tc * 512: pc * 1024 + tc * 512 + 512]
                                w = None
                                ws = ()
                                if pc == 0:
                                    w = mq3
                                    ws = (m_hl,) if idx == 0 else ()
                                    if idx >= 8:
                                        ws = ws + ((s_dve, m_f1[idx % 8][1]),)
                                E("ten", lambda e, p=bank, a=la, b=rh, st=(pc == 0):
                                  e.matmul(p, a, b, start=st, stop=False),
                                  wait=w, waits=ws)
                                E("ten", lambda e, p=bank, a=la, b=rl, sp=(pc == 1):
                                  e.matmul(p, a, b, start=False, stop=sp),
                                  inc=(s_ten, 1) if pc == 1 else None)
                            mmk = s_ten.mark()
                            xsl = xb[:, oc * TOK + tc * 512: oc * TOK + tc * 512 + 512]
                            m_f1[idx % 8] = E("dve", lambda e, o=xsl, p=bank:
                                              e.tensor_tensor(o, o, p, ALU.add),
                                              wait=mmk, inc=(s_dve, 1))
                            m_f1_last = m_f1[idx % 8]
                    m_q3_used = s_ten.mark()
                    if hc + 1 < NBC:
                        pre_q[hc + 1] = (
                            quantize(q1c, w1q_h, hc + 1, 0, mm1[3]),
                            quantize(q2c, w2q_h, hc + 1, 1, mm2[3]))
                return dict(ten=s_ten.mark(), act=s_act.mark(), dve=m_f1_last,
                            scl=s_scl.mark())

            _PHASE[0] = "bl"
            bl = emit_bitlin(kv1)
        if stage_ge("full"):
            _PHASE[0] = "heads1"
            l1_done = emit_heads(1, [wq1_h[i] for i in range(NQH)],
                                 [wd1_h[i] for i in range(NQH)],
                                 kvr[1], False, False, False, kv1["cc"], bl)
            fin = l1_done
        else:
            fin = l0_done if l0_done is not None else dict(
                ten=s_ten.mark(), act=s_act.mark(), dve=s_dve.mark(), scl=s_scl.mark())

        m_fin_d = s_dve.mark()
        m_fin_a = s_act.mark()
        m_fin_t = s_ten.mark()
        mo = dma("syn", out_h, xb, wait=m_fin_d,
                 waits=tuple(m for m in (m_fin_a, m_fin_t) if m[1] > 0))

        with nc.Block() as block:
            @block.sync
            def _(sync):
                P.run("syn", sync)
                sync.wait_ge(s_syn.h, mo[1])

            @block.tensor
            def _(tensor):
                P.run("ten", tensor)

            @block.scalar
            def _(scalar):
                P.run("act", scalar)

            @block.vector
            def _(vector):
                P.run("dve", vector)

            @block.gpsimd
            def _(gpsimd):
                P.run("gps", gpsimd)

    return nc


def _rope_tables(tok0):
    i = np.arange(1, 65, dtype=np.float32)
    thetas = (np.float32(10000.0) ** ((np.float32(-2.0) * (i - np.float32(1.0)))
                                      / np.float32(128.0))).astype(np.float32)
    pos = np.arange(tok0 + 1, tok0 + TOK + 1, dtype=np.float32)
    ang = (pos[:, None] * thetas[None, :]).astype(np.float32)
    ang64 = ang.astype(np.float64)
    cos = np.cos(ang64).astype(np.float32)
    sin = np.sin(ang64).astype(np.float32)
    ctab = np.empty((128, TOK), np.float32)
    stab = np.empty((128, TOK), np.float32)
    ctab[0:64] = cos.T
    ctab[64:128] = cos.T
    stab[0:64] = -sin.T
    stab[64:128] = sin.T
    return ctab, stab


def _pack_bias(b, nch):
    return np.ascontiguousarray(np.asarray(b, np.float32).reshape(nch, 128).T)


_W_CACHE = {}


def _prep_weights(inputs):
    key = id(inputs.get("L0_attn_Wq"))
    if key in _W_CACHE:
        return _W_CACHE[key]
    f = np.float32
    cc = np.ascontiguousarray
    w = {}
    WqT = np.asarray(inputs["L0_attn_Wq"], f).T
    w["wq0"] = cc(WqT.reshape(16, 128, 16, 128).transpose(2, 1, 0, 3).reshape(16, 128, 2048))
    w["wk0"] = cc(np.asarray(inputs["L0_attn_Wk"], f).T
                  .reshape(4, 4, 128, 512).transpose(0, 2, 1, 3).reshape(4, 128, 2048))
    w["wv0"] = cc(np.asarray(inputs["L0_attn_Wv"], f).T
                  .reshape(4, 4, 128, 512).transpose(0, 2, 1, 3).reshape(4, 128, 2048))
    w["wo0"] = cc(np.asarray(inputs["L0_attn_Wo"], f).T.reshape(16, 128, 2048))
    w["w00"] = cc(np.asarray(inputs["L0_ffn_W0"], f).T
                  .reshape(2, 8, 128, 16, 512).transpose(3, 0, 2, 1, 4).reshape(16, 2, 128, 4096))
    w["w10"] = cc(np.asarray(inputs["L0_ffn_W1"], f).T
                  .reshape(16, 2, 2, 128, 2048).transpose(0, 1, 3, 2, 4).reshape(16, 2, 128, 4096))
    w["wq1"] = cc(np.asarray(inputs["L1_attn_w_q"], f)
                  .reshape(16, 128, 16, 128).transpose(2, 1, 0, 3).reshape(16, 128, 2048))
    wkv = np.asarray(inputs["L1_attn_w_kv"], f)
    w["wk1"] = cc(wkv[:, 0:512].reshape(4, 4, 128, 512).transpose(0, 2, 1, 3).reshape(4, 128, 2048))
    w["wv1"] = cc(wkv[:, 512:1024].reshape(4, 4, 128, 512).transpose(0, 2, 1, 3).reshape(4, 128, 2048))
    w["wd1"] = cc(np.asarray(inputs["L1_attn_w_dense"], f).reshape(16, 128, 2048))
    w1 = np.asarray(inputs["L1_ffn_w1"], f)
    w2 = np.asarray(inputs["L1_ffn_w2"], f)
    w3 = np.asarray(inputs["L1_ffn_w3"], f)
    w["w1q"] = cc(w1.T.reshape(2, 8, 128, 22, 256).transpose(3, 0, 2, 1, 4).reshape(22, 2, 128, 2048))
    w["w2q"] = cc(w2.T.reshape(2, 8, 128, 22, 256).transpose(3, 0, 2, 1, 4).reshape(22, 2, 128, 2048))
    w["w3q"] = cc(w3.T.reshape(22, 2, 128, 2048))
    w["bq_pk"] = _pack_bias(inputs["L0_attn_bq"], 16)
    w["bk_pk"] = _pack_bias(inputs["L0_attn_bk"], 4)
    w["bv_b"] = cc(np.broadcast_to(np.asarray(inputs["L0_attn_bv"], f), (128, 512)))
    w["bo_pk"] = _pack_bias(inputs["L0_attn_bo"], 16)
    w["b0_pk"] = _pack_bias(inputs["L0_ffn_bias0"], 64)
    w["b1_pk"] = _pack_bias(inputs["L0_ffn_bias1"], 16)
    w["kidx"] = cc((np.arange(16)[None, :] * 128 + np.arange(128)[:, None]).astype(f))
    N8 = HID1 * D // 8
    w["_gsrc"] = []
    for c in range(8):
        parts = [m.ravel()[c * N8:(c + 1) * N8].reshape(22, 128, 512)
                 for m in (w1, w2, w3)]
        w["_gsrc"].append(cc(np.concatenate(parts, axis=0)))
    _W_CACHE[key] = w
    return w


def kernel(**inputs):
    w = _prep_weights(inputs)
    x = np.asarray(inputs["L0_attn_x"], dtype=np.float32)
    base = {k: v for k, v in w.items() if not k.startswith("_")}
    in_maps = []
    for c in range(8):
        b, half = c // 2, c % 2
        tok0 = half * TOK
        xs = np.ascontiguousarray(
            x[b, tok0:tok0 + TOK, :].T.reshape(16, 128, TOK).transpose(1, 0, 2)
            .reshape(128, 16 * TOK))
        ctab, stab = _rope_tables(tok0)
        m = dict(base)
        m["x0t"] = xs
        m["qidxb"] = np.ascontiguousarray(
            np.broadcast_to(np.arange(tok0, tok0 + TOK, dtype=np.float16), (128, TOK)))
        m["ropecos"] = ctab
        m["ropesin"] = stab
        m["gsrc"] = w["_gsrc"][c]
        in_maps.append(m)
    nc = build_nc()
    res = run_bass_kernel_spmd(nc, in_maps, core_ids=list(range(8)))
    out = np.empty((B, T, D), dtype=np.float32)
    for c in range(8):
        b, half = c // 2, c % 2
        o = np.asarray(res.results[c]["out"])
        out[b, half * TOK:(half + 1) * TOK, :] = (
            o.reshape(128, 16, TOK).transpose(1, 0, 2).reshape(D, TOK).T)
    return out


# version 09
# speedup vs baseline: 1.0257x; 1.0257x over previous
import math
import numpy as np
from contextlib import ExitStack

import concourse.bass as bass
import concourse.mybir as mybir
from concourse.bass_utils import run_bass_kernel_spmd

F32 = mybir.dt.float32
F16 = mybir.dt.float16
ALU = mybir.AluOpType
AF = mybir.ActivationFunctionType

B, T, D = 4, 2048, 2048
TOK = 1024
DCH = 16
NQH = 16
NGRP = 4
HID1 = 5632
NBC = 22
SCALE = float(np.float32(128.0) ** np.float32(-0.5))
NEG = -30000.0
NKTR = {0: (8, 4), 1: (8, 8)}

_STAGE = "full"
_TAPS = False
_FFNC = 16
_PHASE = ["?"]
_TENLBL = []


class Sem:
    def __init__(self, h):
        self.h = h
        self.n = 0

    def mark(self):
        return (self, self.n)


class Prog:
    ENGS = ("syn", "ten", "act", "dve", "gps")

    def __init__(self):
        self.q = {e: [] for e in self.ENGS}

    def emit(self, eng, builder, wait=None, waits=(), inc=None):
        if eng == "ten":
            _TENLBL.append(_PHASE[0])
        mk = None
        if inc is not None:
            s, k = inc
            s.n += k
            mk = (s, s.n)
        self.q[eng].append((builder, wait, tuple(w for w in waits if w is not None), inc))
        return mk

    def run(self, eng, engine):
        for builder, wait, waits, inc in self.q[eng]:
            for (s, v) in waits:
                if v > 0:
                    engine.wait_ge(s.h, v)
            ins = builder(engine)
            if wait is not None and wait[1] > 0:
                ins._wait_ge(wait[0].h, wait[1])
            if inc is not None:
                ins.then_inc(inc[0].h, inc[1])


class Bump:
    def __init__(self, ap_full, cols, name):
        self.t = ap_full
        self.cols = cols
        self.name = name
        self.off = 0

    def take(self, n):
        assert self.off + n <= self.cols, f"{self.name} overflow {self.off}+{n}>{self.cols}"
        o = self.off
        self.off += n
        return self.t[:, o:o + n]


def build_nc():
    nc = bass.Bass("TRN2", target_bir_lowering=False)
    P = Prog()

    def din(name, shape, dt=F32):
        return nc.dram_tensor(name, list(shape), dt, kind="ExternalInput").ap()

    x0t = din("x0t", (128, DCH * TOK))
    qidx_h = din("qidxb", (128, TOK), F16)
    kidx_h = din("kidx", (128, 16))
    cos_h = din("ropecos", (128, TOK))
    sin_h = din("ropesin", (128, TOK))
    bq_h = din("bq_pk", (128, 16))
    bk_h = din("bk_pk", (128, 4))
    bvb_h = din("bv_b", (128, 512))
    bo_h = din("bo_pk", (128, 16))
    b0_h = din("b0_pk", (128, 64))
    b1_h = din("b1_pk", (128, 16))
    wq0_h = din("wq0", (NQH, 128, 2048))
    wk0_h = din("wk0", (4, 128, 2048))
    wv0_h = din("wv0", (4, 128, 2048))
    wo0_h = din("wo0", (NQH, 128, D))
    w00_h = din("w00", (16, 2, 128, 4096))
    w10_h = din("w10", (16, 2, 128, 4096))
    wq1_h = din("wq1", (NQH, 128, 2048))
    wk1_h = din("wk1", (4, 128, 2048))
    wv1_h = din("wv1", (4, 128, 2048))
    wd1_h = din("wd1", (NQH, 128, D))
    w1q_h = din("w1q", (NBC, 2, 128, 2048))
    w2q_h = din("w2q", (NBC, 2, 128, 2048))
    w3q_h = din("w3q", (NBC, 2, 128, 2048))
    gsrc_h = din("gsrc", (66, 128, 512))

    out_h = nc.dram_tensor("out", [128, DCH * TOK], F32, kind="ExternalOutput").ap()
    taps = {}
    if _TAPS:
        taps["xnh"] = nc.dram_tensor("tap_xnh", [128, DCH * TOK], F16, kind="ExternalOutput").ap()
        taps["xnl"] = nc.dram_tensor("tap_xnl", [128, DCH * TOK], F16, kind="ExternalOutput").ap()
        taps["kv"] = nc.dram_tensor("tap_kv", [2, 2, NGRP, 128, 2048], F16, kind="ExternalOutput").ap()
        taps["d16"] = nc.dram_tensor("tap_d16", [128, 4096], F16, kind="ExternalOutput").ap()
        taps["thr"] = nc.dram_tensor("tap_thr", [128, 8], F32, kind="ExternalOutput").ap()
        taps["gp"] = nc.dram_tensor("tap_gp", [128, 72], F32, kind="ExternalOutput").ap()
        taps["ga"] = nc.dram_tensor("tap_ga", [128, 24], F32, kind="ExternalOutput").ap()

    kvs = [nc.dram_tensor(f"kv_send{L}", [2, NGRP, 128, 2048], F16, kind="Internal").ap()
           for L in range(2)]
    kvr = [nc.dram_tensor(f"kv_recv{L}", [2, 2, NGRP, 128, 2048], F16, kind="Internal").ap()
           for L in range(2)]
    g_snd = nc.dram_tensor("g_snd", [128, 3], F32, kind="Internal").ap()
    g_rcv = nc.dram_tensor("g_rcv", [8, 128, 3], F32, kind="Internal").ap()
    RG2 = [[0, 1], [2, 3], [4, 5], [6, 7]]
    RG8 = [[0, 1, 2, 3, 4, 5, 6, 7]]

    es = ExitStack()
    with es:
        sb = lambda name, shape, dt: es.enter_context(nc.sbuf_tensor(name, list(shape), dt))[:, :]
        ps = lambda name: es.enter_context(nc.psum_tensor(name, [128, 512], F32))
        sem = lambda name: Sem(es.enter_context(nc.semaphore(name)))

        xb = sb("xb", (128, DCH * TOK), F32)
        xnh = sb("xnh_s", (128, DCH * TOK), F16)
        xnl = sb("xnl_s", (128, DCH * TOK), F16)
        a32 = sb("a32", (128, 6144), F32)
        a16 = sb("a16", (128, 20480), F16)
        c_bq = sb("c_bq", (128, 16), F32)
        c_bk = sb("c_bk", (128, 4), F32)
        c_bvb = sb("c_bvb", (128, 512), F32)
        c_bo = sb("c_bo", (128, 16), F32)
        c_b0 = sb("c_b0", (128, 64), F32)
        c_b1 = sb("c_b1", (128, 16), F32)
        c_cos = sb("c_cos", (128, TOK), F32)
        c_sin = sb("c_sin", (128, TOK), F32)
        c_kidx = sb("c_kidx", (128, 16), F32)
        qidxb = sb("qidxb_s", (128, TOK), F16)
        ones_r = sb("ones_r", (1, 128), F32)
        ones_c32 = sb("ones_c32", (128, 1), F32)
        ones_c16 = sb("ones_c16", (128, 1), F16)
        g_stg = sb("g_stg", (128, 512), F32)
        g_part = sb("g_part", (128, 72), F32)
        g_all = sb("g_all", (128, 24), F32)
        g_thr = sb("g_thr", (128, 8), F32)
        c_eps = sb("c_eps", (128, 1), F32)
        c_geps = sb("c_geps", (128, 1), F32)

        pb = [ps(f"pb{i}") for i in range(8)]
        pba = [p[:, :] for p in pb]

        s_syn = sem("s_syn")
        s_scl = sem("s_scl")
        s_gpd = sem("s_gpd")
        s_cc = sem("s_cc")
        s_ten = sem("s_ten")
        s_act = sem("s_act")
        s_dve = sem("s_dve")
        s_gps = sem("s_gps")

        E = P.emit

        def dma(eng, out, in_, wait=None, waits=()):
            s = {"syn": s_syn, "act": s_scl, "gps": s_gpd}[eng]
            return E(eng, lambda e, o=out, i=in_: e.dma_start(out=o, in_=i),
                     wait=wait, waits=waits, inc=(s, 16))

        m_x0q = []
        for qq in range(4):
            m_x0q.append(dma("syn", xb[:, qq * 4096:(qq + 1) * 4096],
                             x0t[:, qq * 4096:(qq + 1) * 4096]))
        for dst, src in ((c_bq, bq_h), (c_bk, bk_h), (c_bvb, bvb_h), (c_bo, bo_h),
                         (c_b0, b0_h), (c_b1, b1_h), (c_cos, cos_h), (c_sin, sin_h),
                         (c_kidx, kidx_h), (qidxb, qidx_h)):
            m_init = dma("syn", dst, src)
        E("dve", lambda e: e.memset(ones_r, 1.0), inc=(s_dve, 1))
        E("dve", lambda e: e.memset(ones_c32, 1.0), inc=(s_dve, 1))
        E("dve", lambda e: e.memset(c_eps, 1e-5), inc=(s_dve, 1))
        E("dve", lambda e: e.memset(c_geps, 0.5e-8), inc=(s_dve, 1))
        m_ones = E("dve", lambda e: e.memset(ones_c16, 1.0), inc=(s_dve, 1))

        def emit_ln(tag, entry_d, entry_ws, entry_per_dc=None):
            A = Bump(a32, 6144, f"ln-{tag}")
            sq0 = A.take(1024)
            sq1 = A.take(1024)
            mrowb = A.take(1024)
            vrowb = A.take(1024)
            m_b = A.take(1024)
            r_b = A.take(1024)
            mrow = mrowb[0:1, :]
            vrow = vrowb[0:1, :]

            m_sq = [None, None]
            m_var_mm = [None, None]
            for dc in range(DCH):
                par = dc % 2
                sqb = (sq0, sq1)[par]
                wsx = entry_ws if dc == 0 else ()
                if entry_per_dc is not None and dc % 4 == 0:
                    wsx = wsx + (entry_per_dc[dc // 4],)
                m_sq[par] = E("act",
                              lambda e, o=sqb, i=xb[:, dc * TOK:(dc + 1) * TOK]:
                              e.activation(out=o, in_=i, func=AF.Square),
                              wait=m_var_mm[par] if dc >= 2 else
                              (None if entry_per_dc is not None else entry_d),
                              waits=wsx, inc=(s_act, 1))
                for tc in range(2):
                    E("ten", lambda e, p=pba[tc][0:1, :],
                      r=xb[:, dc * TOK + tc * 512: dc * TOK + tc * 512 + 512],
                      st=(dc == 0), sp=(dc == DCH - 1):
                      e.matmul(p, ones_c32, r, start=st, stop=sp),
                      wait=entry_d if (dc == 0 and tc == 0 and entry_per_dc is None) else None,
                      waits=((entry_ws + ((m_ones,) if m_ones else ())) if (dc == 0 and tc == 0) else ())
                      + (wsx if (entry_per_dc is not None and dc % 4 == 0 and tc == 0) else ()),
                      inc=(s_ten, 1) if dc == DCH - 1 else None)
                for tc in range(2):
                    m_var_mm[par] = E("ten", lambda e, p=pba[2 + tc][0:1, :],
                                      r=sqb[:, tc * 512:tc * 512 + 512],
                                      st=(dc == 0), sp=(dc == DCH - 1):
                                      e.matmul(p, ones_c32, r, start=st, stop=sp),
                                      wait=m_sq[par] if tc == 0 else None,
                                      inc=(s_ten, 1))
            m_sums = s_ten.mark()
            mr = None
            for tc in range(2):
                mr = E("act", lambda e, o=mrow[:, tc * 512:tc * 512 + 512], p=pba[tc][0:1, :]:
                       e.activation(out=o, in_=p, func=AF.Identity, scale=1.0 / D),
                       wait=m_sums if tc == 0 else None, inc=(s_act, 1))
                mr = E("act", lambda e, o=vrow[:, tc * 512:tc * 512 + 512], p=pba[2 + tc][0:1, :]:
                       e.activation(out=o, in_=p, func=AF.Identity, scale=1.0 / D),
                       inc=(s_act, 1))
            m_rows = mr
            mv = E("dve", lambda e: e.tensor_tensor(sq0[0:1, :], mrow, mrow, ALU.mult),
                   wait=m_rows, inc=(s_dve, 1))
            mv = E("dve", lambda e: e.tensor_tensor(vrow, vrow, sq0[0:1, :], ALU.subtract),
                   inc=(s_dve, 1))
            mstd = E("act", lambda e: e.activation(out=sq1[0:1, :], in_=vrow,
                                                   func=AF.Sqrt, bias=c_eps[0:1, :]),
                     wait=mv, inc=(s_act, 1))
            mrcp = E("dve", lambda e: e.reciprocal(out=vrow, in_=sq1[0:1, :]),
                     wait=mstd, inc=(s_dve, 1))
            bms = None
            for tc in range(2):
                b1 = E("ten", lambda e, p=pba[4][:, :], r=mrow[:, tc * 512:tc * 512 + 512]:
                       e.matmul(p, ones_r, r, start=True, stop=True),
                       wait=mrcp if tc == 0 else None,
                       waits=(bms,) if tc == 1 else (), inc=(s_ten, 1))
                bms = E("act", lambda e, o=m_b[:, tc * 512:tc * 512 + 512], p=pba[4][:, :]:
                        e.copy(out=o, in_=p), wait=b1, inc=(s_act, 1))
                b2 = E("ten", lambda e, p=pba[5][:, :], r=vrow[:, tc * 512:tc * 512 + 512]:
                       e.matmul(p, ones_r, r, start=True, stop=True),
                       waits=(bms,) if tc == 1 else (), inc=(s_ten, 1))
                bms = E("act", lambda e, o=r_b[:, tc * 512:tc * 512 + 512], p=pba[5][:, :]:
                        e.copy(out=o, in_=p), wait=b2, inc=(s_act, 1))
            m_bc = bms
            mh = [None, None]
            ml = None
            for dc in range(DCH):
                par = dc % 2
                tb = (sq0, vrowb)[par]
                pbuf = (sq1, mrowb)[par]
                E("dve", lambda e, o=tb, i=xb[:, dc * TOK:(dc + 1) * TOK]:
                  e.tensor_tensor(o, i, m_b, ALU.subtract),
                  wait=m_bc if dc < 2 else None, inc=(s_dve, 1))
                E("dve", lambda e, o=pbuf, i=tb: e.tensor_tensor(o, i, r_b, ALU.mult),
                  wait=mh[par], inc=(s_dve, 1))
                mp = s_dve.mark()
                mh[par] = E("act", lambda e, o=xnh[:, dc * TOK:(dc + 1) * TOK], i=pbuf:
                            e.copy(out=o, in_=i), wait=mp, inc=(s_act, 1))
                ml = E("dve", lambda e, o=xnl[:, dc * TOK:(dc + 1) * TOK], i=pbuf,
                       h2=xnh[:, dc * TOK:(dc + 1) * TOK]:
                       e.tensor_tensor(o, i, h2, ALU.subtract), wait=mh[par], inc=(s_dve, 1))
            return ml, mh[(DCH - 1) % 2], s_ten.mark()

        _PHASE[0] = "ln0"
        m_xn_d, m_xn_a, m_ln_t = emit_ln("L0", None, (m_init,), entry_per_dc=m_x0q)
        if _TAPS and _STAGE in ("ln", "kv", "ffn", "heads"):
            dma("syn", taps["xnh"], xnh, wait=m_xn_d, waits=(m_xn_a,))
            dma("syn", taps["xnl"], xnl, wait=m_xn_d)

        stage_order = ["ln", "kv", "ffn", "heads", "ln1", "bl", "full"]
        sidx = stage_order.index(_STAGE)

        def stage_ge(s):
            return stage_order.index(s) <= sidx

        def emit_kv(L, wk_h4, wv_h4, kv_send, rope, has_kb, has_vb,
                    m_xn_d, m_xn_a, prev_ten):
            A = Bump(a32, 6144, f"kv-{L}")
            A16 = Bump(a16, 20480, f"kv16-{L}")
            wstg = A.take(2048)
            kx32 = A.take(1024)
            shuf = A.take(1024)
            wwh = [A16.take(4096), A16.take(4096)]
            wwl = [A16.take(4096), A16.take(4096)]
            khi = A16.take(1024)
            klo = A16.take(1024)

            def stage(src4, guards_ten, m_stg_dve, first_ws=()):
                m_split = [None, None]
                for hf in range(2):
                    for qq in range(2):
                        idx = hf * 2 + qq
                        md = dma("syn", wstg, src4[idx], wait=m_stg_dve,
                                 waits=first_ws if idx == 0 else ())
                        ma = E("act", lambda e, o=wwh[hf][:, qq * 2048:qq * 2048 + 2048], i=wstg:
                               e.copy(out=o, in_=i), wait=md,
                               waits=(guards_ten[hf],) if (qq == 0 and guards_ten[hf]) else (),
                               inc=(s_act, 1))
                        m_stg_dve = E("dve", lambda e, o=wwl[hf][:, qq * 2048:qq * 2048 + 2048],
                                      i=wstg, h2=wwh[hf][:, qq * 2048:qq * 2048 + 2048]:
                                      e.tensor_tensor(o, i, h2, ALU.subtract),
                                      wait=ma, inc=(s_dve, 1))
                    m_split[hf] = m_stg_dve
                return m_split, m_stg_dve

            m_split, m_stg_dve = stage(wk_h4, (prev_ten, prev_ten), m_xn_d,
                                       first_ws=(m_xn_a,))

            m_khalf0 = None
            m_kmm = [None] * 8
            for hf in range(2):
                for kc in range(4):
                    for tc in range(2):
                        bank = pba[kc * 2 + tc]
                        for j in range(8):
                            dc = hf * 8 + j
                            lw = wwh[hf][:, j * 512 + kc * 128: j * 512 + kc * 128 + 128]
                            ll = wwl[hf][:, j * 512 + kc * 128: j * 512 + kc * 128 + 128]
                            rh = xnh[:, dc * TOK + tc * 512: dc * TOK + tc * 512 + 512]
                            rl = xnl[:, dc * TOK + tc * 512: dc * TOK + tc * 512 + 512]
                            w = m_split[hf] if (kc == 0 and tc == 0 and j == 0) else None
                            st = (hf == 0 and j == 0)
                            sp = (hf == 1 and j == 7)
                            E("ten", lambda e, p=bank, a=lw, b=rh, st=st:
                              e.matmul(p, a, b, start=st, stop=False), wait=w)
                            E("ten", lambda e, p=bank, a=lw, b=rl:
                              e.matmul(p, a, b, start=False, stop=False))
                            last = (hf == 1 and kc == 3 and tc == 1 and j == 7)
                            h0last = (hf == 0 and kc == 3 and tc == 1 and j == 7)
                            mk = E("ten", lambda e, p=bank, a=ll, b=rh, sp=sp:
                                   e.matmul(p, a, b, start=False, stop=sp),
                                   inc=(s_ten, 1) if (sp or h0last) else None)
                            if h0last:
                                m_khalf0 = mk
                            if sp:
                                m_kmm[kc * 2 + tc] = mk
            m_kmm_last = s_ten.mark()

            m_kevac = [None] * 8
            last_send = None
            prev_rope = None
            prev_dve = None
            prev_act = None
            for kc in range(4):
                me = None
                for tc in range(2):
                    ws = ()
                    if tc == 0 and kc > 0:
                        ws = ((s_scl, last_send[1]),)
                        if prev_dve:
                            ws = ws + ((s_dve, prev_dve[1]),)
                    if tc == 0 and kc == 0:
                        ws = ((s_syn, m_init[1]),)
                    osl = kx32[:, tc * 512:tc * 512 + 512]
                    if has_kb:
                        me = E("act", lambda e, o=osl, p=pba[kc * 2 + tc], bi=c_bk[:, kc:kc + 1]:
                               e.activation(out=o, in_=p, func=AF.Identity, bias=bi),
                               wait=m_kmm[kc * 2 + tc], waits=ws, inc=(s_act, 1))
                    else:
                        me = E("act", lambda e, o=osl, p=pba[kc * 2 + tc]:
                               e.copy(out=o, in_=p),
                               wait=m_kmm[kc * 2 + tc], waits=ws, inc=(s_act, 1))
                    m_kevac[kc * 2 + tc] = me
                if rope:
                    d1 = dma("act", shuf[0:64, :], kx32[64:128, :], wait=me,
                             waits=((s_dve, prev_rope[1]),) if prev_rope else ())
                    d2 = dma("act", shuf[64:128, :], kx32[0:64, :])
                    E("dve", lambda e: e.tensor_tensor(shuf, shuf, c_sin, ALU.mult),
                      wait=d2, inc=(s_dve, 1))
                    E("dve", lambda e: e.tensor_tensor(kx32, kx32, c_cos, ALU.mult),
                      inc=(s_dve, 1))
                    mr3 = E("dve", lambda e: e.tensor_tensor(kx32, kx32, shuf, ALU.add),
                            inc=(s_dve, 1))
                    prev_rope = mr3
                    wsrc = mr3
                else:
                    wsrc = None
                mhk = E("act", lambda e: e.copy(out=khi, in_=kx32), wait=wsrc,
                        waits=((s_scl, last_send[1]),) if last_send else (), inc=(s_act, 1))
                mlk = E("dve", lambda e: e.tensor_tensor(klo, kx32, khi, ALU.subtract),
                        wait=mhk, inc=(s_dve, 1))
                prev_dve = mlk
                prev_act = mhk
                dma("act", kv_send[0, kc, :, 0:1024], khi, wait=mhk)
                last_send = dma("act", kv_send[1, kc, :, 0:1024], klo, wait=mlk)

            m_vsplit, m_stg_dve = stage(wv_h4, (m_khalf0, m_kmm_last), m_stg_dve)
            m_vmm = [None] * 8
            for hf in range(2):
                for tt in range(8):
                    bank = pba[tt]
                    for j in range(8):
                        dc = hf * 8 + j
                        lh = xnh[:, dc * TOK + tt * 128: dc * TOK + tt * 128 + 128]
                        llo = xnl[:, dc * TOK + tt * 128: dc * TOK + tt * 128 + 128]
                        rh = wwh[hf][:, j * 512: j * 512 + 512]
                        rl = wwl[hf][:, j * 512: j * 512 + 512]
                        w = m_vsplit[hf] if (tt == 0 and j == 0) else None
                        ws = ((s_act, m_kevac[tt][1]),) if (hf == 0 and j == 0) else ()
                        st = (hf == 0 and j == 0)
                        sp = (hf == 1 and j == 7)
                        E("ten", lambda e, p=bank, a=lh, b=rh, st=st:
                          e.matmul(p, a, b, start=st, stop=False), wait=w, waits=ws)
                        E("ten", lambda e, p=bank, a=lh, b=rl:
                          e.matmul(p, a, b, start=False, stop=False))
                        mk = E("ten", lambda e, p=bank, a=llo, b=rh, sp=sp:
                               e.matmul(p, a, b, start=False, stop=sp),
                               inc=(s_ten, 1) if sp else None)
                        if sp:
                            m_vmm[tt] = mk
            vhi = khi
            vlo = klo
            v32 = kx32[:, 0:512]
            for tt in range(8):
                ws = ()
                if tt == 0:
                    ws = (((s_act, prev_act[1]),) if prev_act else ()) + \
                         (((s_scl, last_send[1]),) if last_send else ())
                if has_vb:
                    mve = E("dve", lambda e, o=v32, p=pba[tt]:
                            e.tensor_tensor(o, p, c_bvb, ALU.add),
                            wait=m_vmm[tt], waits=ws, inc=(s_dve, 1))
                else:
                    mve = E("dve", lambda e, o=v32, p=pba[tt]: e.tensor_copy(out=o, in_=p),
                            wait=m_vmm[tt], waits=ws, inc=(s_dve, 1))
                mvh = E("act", lambda e, o=vhi[:, 0:512], i=v32: e.copy(out=o, in_=i),
                        wait=mve, waits=((s_scl, last_send[1]),), inc=(s_act, 1))
                mvl = E("dve", lambda e, o=vlo[:, 0:512], i=v32, h2=vhi[:, 0:512]:
                        e.tensor_tensor(o, i, h2, ALU.subtract), wait=mvh, inc=(s_dve, 1))
                for g in range(4):
                    dma("act", kv_send[0, g, :, 1024 + tt * 128: 1024 + (tt + 1) * 128],
                        vhi[:, g * 128:g * 128 + 128], wait=mvh if g == 0 else None)
                for g in range(4):
                    last_send = dma("act", kv_send[1, g, :, 1024 + tt * 128: 1024 + (tt + 1) * 128],
                                    vlo[:, g * 128:g * 128 + 128], wait=mvl if g == 0 else None)
            mcc = E("gps", lambda e: e.collective_compute(
                "AllGather", ALU.bypass, replica_groups=RG2,
                ins=[kv_send[:, :, :, :]], outs=[kvr[L][:, :, :, :]]),
                wait=last_send, inc=(s_cc, 1))
            return dict(cc=mcc, ten=s_ten.mark(), act=s_act.mark(),
                        dve=s_dve.mark(), scl=last_send)

        kv0 = None
        if stage_ge("kv"):
            _PHASE[0] = "kv0"
            kv0 = emit_kv(0, [wk0_h[i] for i in range(4)], [wv0_h[i] for i in range(4)],
                          kvs[0], True, True, True, m_xn_d, m_xn_a, None)
            if _TAPS:
                dma("syn", taps["kv"], kvr[0][:, :, :, :, :], wait=None,
                    waits=((s_cc, kv0["cc"][1]),))

        gam = {"dma": None, "red": None, "pc": 0}

        def gamma_pump(nmax):
            while gam["pc"] < 66 and nmax > 0:
                pc = gam["pc"]
                gam["dma"] = dma("gps", g_stg, gsrc_h[pc], wait=gam["red"])
                gam["red"] = E("dve", lambda e, o=g_part[:, pc:pc + 1]:
                               e.tensor_reduce(o, g_stg, mybir.AxisListType.X,
                                               ALU.add, apply_absolute_value=True),
                               wait=gam["dma"], inc=(s_dve, 1))
                gam["pc"] += 1
                nmax -= 1

        def emit_ffn0(prev):
            A = Bump(a32, 6144, "ffn")
            A16 = Bump(a16, 20480, "ffn16")
            stg = A.take(4096)
            wbh = A16.take(8192)
            wbl = A16.take(8192)
            d16 = A16.take(4096)

            def half(buf, hf):
                return buf[:, hf * 4096:hf * 4096 + 4096]

            st8 = {"dve": prev["dve"]}

            def stage_half(src_ap, hf, gate_ten, extra_ws=()):
                md = dma("syn", stg, src_ap, wait=st8["dve"], waits=extra_ws)
                ma = E("act", lambda e, o=half(wbh, hf), i=stg:
                       e.copy(out=o, in_=i), wait=md,
                       waits=((s_ten, gate_ten[1]),), inc=(s_act, 1))
                ms = E("dve", lambda e, o=half(wbl, hf), i=stg, h2=half(wbh, hf):
                       e.tensor_tensor(o, i, h2, ALU.subtract),
                       wait=ma, inc=(s_dve, 1))
                st8["dve"] = ms
                return ma, ms

            mE0_prev = prev["ten"]
            mE1_prev = prev["ten"]
            m_evac_last = prev["dve"]
            m_evac = [prev["dve"]] * 8
            first_gelu = True
            for hc in range(_FFNC):
                if stage_ge("bl") and hc >= 3:
                    gamma_pump(6)
                ws0 = ()
                if hc == 0:
                    ws0 = ((s_act, prev["act"][1]), (s_scl, prev["scl"][1]))
                m_sp_w0 = [None, None]
                m_sp_w0[0] = stage_half(w00_h[hc, 0], 0, mE0_prev, ws0)
                m_sp_w0[1] = stage_half(w00_h[hc, 1], 1, mE1_prev)
                m_dmm = [None] * 8
                mD0 = None
                for hf in range(2):
                    for hc2 in range(4):
                        for tc in range(2):
                            bank = pba[hc2 * 2 + tc]
                            for j in range(8):
                                dc = hf * 8 + j
                                lwh = wbh[:, dc * 512 + hc2 * 128: dc * 512 + hc2 * 128 + 128]
                                lwl = wbl[:, dc * 512 + hc2 * 128: dc * 512 + hc2 * 128 + 128]
                                rh = xnh[:, dc * TOK + tc * 512: dc * TOK + tc * 512 + 512]
                                first_i = (hc2 == 0 and tc == 0 and j == 0)
                                w = m_sp_w0[hf][0] if first_i else None
                                ws = ()
                                if hf == 0 and first_i:
                                    ws = ((s_dve, m_evac_last[1]),)
                                st = (hf == 0 and j == 0)
                                sp = (hf == 1 and j == 7)
                                h0last = (hf == 0 and hc2 == 3 and tc == 1 and j == 7)
                                E("ten", lambda e, p=bank, a=lwh, b=rh, st=st:
                                  e.matmul(p, a, b, start=st, stop=False), wait=w, waits=ws)
                                mk = E("ten", lambda e, p=bank, a=lwl, b=rh, sp=sp:
                                       e.matmul(p, a, b, start=False, stop=sp),
                                       waits=(m_sp_w0[hf][1],) if first_i else (),
                                       inc=(s_ten, 1) if (sp or h0last) else None)
                                if h0last:
                                    mD0 = mk
                                if sp:
                                    m_dmm[hc2 * 2 + tc] = mk
                    if hf == 0:
                        m_sp_w1h0 = stage_half(w10_h[hc, 0], 0, mD0)
                m_dmm_last = s_ten.mark()
                m_gel = None
                for hc2 in range(4):
                    for tc in range(2):
                        ws = ()
                        if first_gelu:
                            ws = ((s_scl, prev["scl"][1]),)
                            first_gelu = False
                        m_gel = E("act", lambda e,
                                  o=d16[:, hc2 * 1024 + tc * 512: hc2 * 1024 + tc * 512 + 512],
                                  p=pba[hc2 * 2 + tc], bi=c_b0[:, hc * 4 + hc2: hc * 4 + hc2 + 1]:
                                  e.activation(out=o, in_=p, func=AF.Gelu, bias=bi),
                                  wait=m_dmm[hc2 * 2 + tc], waits=ws, inc=(s_act, 1))
                if _TAPS and hc == 0:
                    dma("syn", taps["d16"], d16, wait=m_gel)
                m_sp_w1h1 = stage_half(w10_h[hc, 1], 1, m_dmm_last)
                mE0 = None
                for og in range(4):
                    for pair in range(2):
                        for oc in range(og * 4, og * 4 + 4):
                            for tc in range(2):
                                bank = pba[(oc % 4) * 2 + tc]
                                for pc in (pair * 2, pair * 2 + 1):
                                    lwh = wbh[:, pc * 2048 + oc * 128: pc * 2048 + oc * 128 + 128]
                                    lwl = wbl[:, pc * 2048 + oc * 128: pc * 2048 + oc * 128 + 128]
                                    rh = d16[:, pc * 1024 + tc * 512: pc * 1024 + tc * 512 + 512]
                                    w = None
                                    ws = ()
                                    lws = ()
                                    first_of_grp = (pc == pair * 2)
                                    if first_of_grp and pair == 0:
                                        ws = ((s_dve, m_evac[(oc % 4) * 2 + tc][1]),)
                                        if og == 0 and oc == 0 and tc == 0:
                                            w = m_sp_w1h0[0]
                                            lws = (m_sp_w1h0[1],)
                                            ws = ws + ((s_act, m_gel[1]),)
                                    if first_of_grp and pair == 1 and og == 0 and oc == 4 * og and tc == 0:
                                        w = m_sp_w1h1[0]
                                        lws = (m_sp_w1h1[1],)
                                    st = (pair == 0 and pc == 0)
                                    sp = (pair == 1 and pc == 3)
                                    p0last = (pair == 0 and og == 3 and oc == 15
                                              and tc == 1 and pc == 1)
                                    E("ten", lambda e, p=bank, a=lwh, b=rh, st=st:
                                      e.matmul(p, a, b, start=st, stop=False),
                                      wait=w, waits=ws)
                                    mk2 = E("ten", lambda e, p=bank, a=lwl, b=rh, sp=sp:
                                            e.matmul(p, a, b, start=False, stop=sp),
                                            waits=lws,
                                            inc=(s_ten, 1) if (sp or p0last) else None)
                                    if p0last:
                                        mE0 = mk2
                                if pair == 1:
                                    mmk = s_ten.mark()
                                    xsl = xb[:, oc * TOK + tc * 512: oc * TOK + tc * 512 + 512]
                                    if hc == 0:
                                        mev = E("dve", lambda e, o=xsl, p=bank,
                                                bi=c_b1[:, oc:oc + 1]:
                                                e.scalar_tensor_tensor(o, p, bi, o,
                                                                       ALU.add, ALU.add),
                                                wait=mmk, inc=(s_dve, 1))
                                    else:
                                        mev = E("dve", lambda e, o=xsl, p=bank:
                                                e.tensor_tensor(o, o, p, ALU.add),
                                                wait=mmk, inc=(s_dve, 1))
                                    m_evac[(oc % 4) * 2 + tc] = mev
                                    m_evac_last = mev
                mE0_prev = mE0 if mE0 is not None else s_ten.mark()
                mE1_prev = s_ten.mark()
            return dict(ten=s_ten.mark(), act=s_act.mark(), dve=m_evac_last,
                        scl=s_scl.mark())

        ffn0 = None
        if stage_ge("ffn"):
            _PHASE[0] = "ffn0"
            ffn0 = emit_ffn0(kv0)

        def emit_heads(L, wq_h, wo_h, kvrL, rope, has_qb, has_ob, mcc, prev):
            A = Bump(a32, 6144, f"hd-{L}")
            A16 = Bump(a16, 20480, f"hd16-{L}")
            stg = A.take(2048)
            q32 = A.take(1024)
            shuf = A.take(1024)
            y32 = A.take(1024)
            recipb = A.take(512)
            exp32b = A.take(512)
            exp32s = [q32[:, 0:512], exp32b]
            msk = shuf[:, 0:512]
            wh = A16.take(2048)
            wl = A16.take(2048)
            qhi = A16.take(1024)
            qlo = A16.take(1024)
            yhi = A16.take(1024)
            ylo = A16.take(1024)
            ehi = [A16.take(512) for _ in range(4)]
            elo = [A16.take(512) for _ in range(4)]
            kvbh = [A16.take(2048), A16.take(2048)]
            kvbl = [A16.take(2048), A16.take(2048)]
            recip = recipb[0:1, :]

            m_xadd = [prev["dve"]] * 8
            m_xadd_last = prev["dve"]
            m_wo_ten = prev["ten"]
            m_kvfree = [prev["ten"], prev["ten"]]
            m_stg_dve = prev["dve"]
            prev_rope = None
            prev_ropedma = None
            for g in range(NGRP):
                m_kvload = [None, None]
                for hh in range(4):
                    h = g * 4 + hh
                    ws = ()
                    if h == 0:
                        ws = ((s_act, prev["act"][1]), (s_scl, prev["scl"][1]),
                              (s_ten, prev["ten"][1]))
                    mdq = dma("syn", stg, wq_h[h], wait=m_stg_dve, waits=ws)
                    maq = E("act", lambda e: e.copy(out=wh, in_=stg), wait=mdq,
                            waits=((s_ten, m_wo_ten[1]),), inc=(s_act, 1))
                    msq = E("dve", lambda e: e.tensor_tensor(wl, stg, wh, ALU.subtract),
                            wait=maq, inc=(s_dve, 1))
                    if hh == 0:
                        for r in range(2):
                            wsl = ((s_cc, mcc[1]),) if (h == 0 and r == 0) else ()
                            dma("act", kvbh[r], kvrL[r, 0, g, :, :],
                                wait=(s_ten, m_kvfree[r][1]), waits=wsl)
                            dma("act", kvbl[r], kvrL[r, 1, g, :, :])
                            m_kvload[r] = s_scl.mark()
                    mq = [None, None]
                    for tc in range(2):
                        bank = pba[tc]
                        for dc in range(DCH):
                            lwh = wh[:, dc * 128: dc * 128 + 128]
                            lwl = wl[:, dc * 128: dc * 128 + 128]
                            rh = xnh[:, dc * TOK + tc * 512: dc * TOK + tc * 512 + 512]
                            rl = xnl[:, dc * TOK + tc * 512: dc * TOK + tc * 512 + 512]
                            w = msq if dc == 0 else None
                            ws = ((s_dve, m_xadd[tc][1]),) if dc == 0 else ()
                            E("ten", lambda e, p=bank, a=lwh, b=rh, st=(dc == 0):
                              e.matmul(p, a, b, start=st, stop=False), wait=w, waits=ws)
                            E("ten", lambda e, p=bank, a=lwh, b=rl:
                              e.matmul(p, a, b, start=False, stop=False))
                            mq[tc] = E("ten", lambda e, p=bank, a=lwl, b=rh, sp=(dc == DCH - 1):
                                       e.matmul(p, a, b, start=False, stop=sp),
                                       inc=(s_ten, 1) if dc == DCH - 1 else None)
                    meq = None
                    for tc in range(2):
                        ws = ()
                        if tc == 0 and rope and prev_ropedma is not None:
                            ws = ((s_scl, prev_ropedma[1]),)
                        osl = q32[:, tc * 512:tc * 512 + 512]
                        if has_qb:
                            meq = E("act", lambda e, o=osl, p=pba[tc], bi=c_bq[:, h:h + 1]:
                                    e.activation(out=o, in_=p, func=AF.Identity, bias=bi),
                                    wait=mq[tc], waits=ws, inc=(s_act, 1))
                        else:
                            meq = E("act", lambda e, o=osl, p=pba[tc]: e.copy(out=o, in_=p),
                                    wait=mq[tc], waits=ws, inc=(s_act, 1))
                    if rope:
                        d1 = dma("act", shuf[0:64, :], q32[64:128, :], wait=meq,
                                 waits=((s_dve, prev_rope[1]),) if prev_rope else ())
                        d2 = dma("act", shuf[64:128, :], q32[0:64, :])
                        prev_ropedma = d2
                        E("dve", lambda e: e.tensor_tensor(shuf, shuf, c_sin, ALU.mult),
                          wait=d2, inc=(s_dve, 1))
                        E("dve", lambda e: e.tensor_tensor(q32, q32, c_cos, ALU.mult),
                          inc=(s_dve, 1))
                        mr3 = E("dve", lambda e: e.tensor_tensor(q32, q32, shuf, ALU.add),
                                inc=(s_dve, 1))
                        prev_rope = mr3
                        mqh = E("act", lambda e: e.copy(out=qhi, in_=q32), wait=mr3,
                                inc=(s_act, 1))
                    else:
                        mqh = E("act", lambda e: e.copy(out=qhi, in_=q32), inc=(s_act, 1))
                    mql = E("dve", lambda e: e.tensor_tensor(qlo, q32, qhi, ALU.subtract),
                            wait=mqh, inc=(s_dve, 1))
                    mdo = dma("syn", stg, wo_h[h], wait=msq)
                    mao = E("act", lambda e: e.copy(out=wh, in_=stg), wait=mdo,
                            waits=((s_ten, mq[1][1]),), inc=(s_act, 1))
                    mso = E("dve", lambda e: e.tensor_tensor(wl, stg, wh, ALU.subtract),
                            wait=mao, inc=(s_dve, 1))
                    m_stg_dve = mso
                    tiles = []
                    for r in range(2):
                        for tc in range(2):
                            nk = NKTR[tc][r]
                            for j in range(nk):
                                tiles.append((r, tc, j, r == 1 and j == nk - 1))
                    nt = len(tiles)
                    stop0 = next(t for t, q in enumerate(tiles)
                                 if q[1] == 0 and q[3])
                    lastr0 = max(t for t, q in enumerate(tiles) if q[0] == 0)
                    m_e_act = [None] * 4
                    m_el = [None] * 4
                    m_tile = [None] * 4
                    m_summ = [None, None]
                    m_avmm = [None, None]
                    m_y = [None, None]
                    m_yl = [None, None]
                    sum_started = [False, False]
                    seen_r = [False, False]

                    SB = (2, 3, 0, 1)

                    def emit_scores(t):
                        r, tc, j, _stop = tiles[t]
                        kt = r * 8 + j
                        sp_ = t % 4
                        exp32 = exp32s[t % 2]
                        sbank = pba[SB[sp_]]
                        lkh = kvbh[r][:, j * 128:(j + 1) * 128]
                        lkl = kvbl[r][:, j * 128:(j + 1) * 128]
                        rqh = qhi[:, tc * 512:tc * 512 + 512]
                        rql = qlo[:, tc * 512:tc * 512 + 512]
                        ws = ()
                        if not seen_r[r]:
                            ws = ((s_scl, m_kvload[r][1]),)
                            seen_r[r] = True
                        if m_el[sp_] is not None:
                            ws = ws + ((s_dve, m_el[sp_][1]),)

                        E("ten", lambda e, p=sbank, a=lkh, b=rqh:
                          e.matmul(p, a, b, start=True, stop=False),
                          wait=mql, waits=ws)
                        E("ten", lambda e, p=sbank, a=lkh, b=rql:
                          e.matmul(p, a, b, start=False, stop=False))
                        msc = E("ten", lambda e, p=sbank, a=lkl, b=rqh:
                                e.matmul(p, a, b, start=False, stop=True),
                                inc=(s_ten, 1))
                        mgw = (s_syn, m_init[1]) if t == 0 else None
                        E("dve", lambda e, o=msk,
                          qv=qidxb[:, tc * 512:tc * 512 + 512],
                          kvx=c_kidx[:, kt:kt + 1]:
                          e.tensor_scalar(o, qv, kvx, NEG, ALU.is_lt, ALU.mult),
                          wait=mgw, inc=(s_dve, 1))
                        mev = E("dve", lambda e, p=sbank, x=exp32:
                                e.scalar_tensor_tensor(x, p, SCALE, msk,
                                                       ALU.mult, ALU.add),
                                wait=msc,
                                waits=((s_act, m_e_act[(sp_ + 2) % 4][1]),)
                                if m_e_act[(sp_ + 2) % 4] else (),
                                inc=(s_dve, 1))
                        E("act", lambda e, x=exp32: e.activation(out=x, in_=x,
                                                                 func=AF.Exp),
                          wait=mev, inc=(s_act, 1))
                        mhc = E("act", lambda e, o=ehi[sp_], x=exp32: e.copy(out=o, in_=x),
                                waits=((s_ten, m_tile[sp_][1]),) if m_tile[sp_] else (),
                                inc=(s_act, 1))
                        m_e_act[sp_] = mhc
                        mlc = E("dve", lambda e, o=elo[sp_], h2=ehi[sp_], x=exp32:
                                e.tensor_tensor(o, x, h2, ALU.subtract),
                                wait=mhc, inc=(s_dve, 1))
                        m_el[sp_] = mlc

                    def emit_sumav(t):
                        r, tc, j, sp_stop = tiles[t]
                        sp_ = t % 4
                        sum_bank = pba[6 + tc][0:1, :]
                        st = not sum_started[tc]
                        sum_started[tc] = True
                        E("ten", lambda e, p=sum_bank, a=ehi[sp_], st=st:
                          e.matmul(p, ones_c16, a, start=st, stop=False),
                          wait=m_el[sp_])
                        mk = E("ten", lambda e, p=sum_bank, a=elo[sp_], sp=sp_stop:
                               e.matmul(p, ones_c16, a, start=False, stop=sp),
                               inc=(s_ten, 1) if sp_stop else None)
                        if sp_stop:
                            m_summ[tc] = mk
                        av_bank = pba[4 + tc]
                        lvh = kvbh[r][:, 1024 + j * 128: 1024 + (j + 1) * 128]
                        lvl = kvbl[r][:, 1024 + j * 128: 1024 + (j + 1) * 128]
                        E("ten", lambda e, p=av_bank, a=lvh, b=ehi[sp_], st=st:
                          e.matmul(p, a, b, start=st, stop=False))
                        E("ten", lambda e, p=av_bank, a=lvh, b=elo[sp_]:
                          e.matmul(p, a, b, start=False, stop=False))
                        mk = E("ten", lambda e, p=av_bank, a=lvl, b=ehi[sp_],
                               sp=sp_stop:
                               e.matmul(p, a, b, start=False, stop=sp),
                               inc=(s_ten, 1))
                        m_tile[sp_] = mk
                        if sp_stop:
                            m_avmm[tc] = mk
                        if hh == 3 and t == lastr0:
                            m_kvfree[0] = s_ten.mark()

                    def emit_norm(tc):
                        mrc = E("dve", lambda e, p=pba[6 + tc][0:1, :]:
                                e.reciprocal(out=recip, in_=p),
                                wait=m_summ[tc], inc=(s_dve, 1))
                        mbb = E("ten", lambda e, p=pba[6 + tc]:
                                e.matmul(p, ones_r, recip, start=True, stop=True),
                                wait=mrc, inc=(s_ten, 1))
                        mbe = E("act", lambda e, p=pba[6 + tc]: e.copy(out=recipb, in_=p),
                                wait=mbb, inc=(s_act, 1))
                        m_y[tc] = E("dve", lambda e, o=y32[:, tc * 512:tc * 512 + 512],
                                    p=pba[4 + tc]:
                                    e.tensor_tensor(o, p, recipb, ALU.mult),
                                    wait=m_avmm[tc], waits=(mbe,), inc=(s_dve, 1))
                        myh_t = E("act", lambda e, o=yhi[:, tc * 512:tc * 512 + 512],
                                  i=y32[:, tc * 512:tc * 512 + 512]:
                                  e.copy(out=o, in_=i), wait=m_y[tc], inc=(s_act, 1))
                        m_yl[tc] = E("dve", lambda e, o=ylo[:, tc * 512:tc * 512 + 512],
                                     i=y32[:, tc * 512:tc * 512 + 512],
                                     h2=yhi[:, tc * 512:tc * 512 + 512]:
                                     e.tensor_tensor(o, i, h2, ALU.subtract),
                                     wait=myh_t, inc=(s_dve, 1))

                    DEPTH = 3
                    for t in range(min(DEPTH, nt)):
                        emit_scores(t)
                    norm0_done = False
                    for t in range(DEPTH, nt + DEPTH):
                        if t < nt:
                            emit_scores(t)
                        k = t - DEPTH
                        emit_sumav(k)
                        if k >= stop0 + 2 and not norm0_done:
                            emit_norm(0)
                            norm0_done = True
                    if not norm0_done:
                        emit_norm(0)
                    emit_norm(1)
                    if hh == 3:
                        m_kvfree[1] = s_ten.mark()
                    myl = m_yl[1]
                    for oc in range(16):
                        for tc in range(2):
                            idx = oc * 2 + tc
                            bank = pba[idx % 8]
                            lwh = wh[:, oc * 128: oc * 128 + 128]
                            lwl = wl[:, oc * 128: oc * 128 + 128]
                            ryh = yhi[:, tc * 512:tc * 512 + 512]
                            ryl = ylo[:, tc * 512:tc * 512 + 512]
                            w = myl if idx == 0 else None
                            ws = ((s_dve, m_xadd[idx % 8][1]),) if idx >= 8 else ()
                            E("ten", lambda e, p=bank, a=lwh, b=ryh:
                              e.matmul(p, a, b, start=True, stop=False), wait=w, waits=ws)
                            E("ten", lambda e, p=bank, a=lwh, b=ryl:
                              e.matmul(p, a, b, start=False, stop=False))
                            mwo = E("ten", lambda e, p=bank, a=lwl, b=ryh:
                                    e.matmul(p, a, b, start=False, stop=True),
                                    inc=(s_ten, 1))
                            xsl = xb[:, oc * TOK + tc * 512: oc * TOK + tc * 512 + 512]
                            if has_ob and h == 0:
                                mxa = E("dve", lambda e, o=xsl, p=bank, bi=c_bo[:, oc:oc + 1]:
                                        e.scalar_tensor_tensor(o, p, bi, o, ALU.add, ALU.add),
                                        wait=mwo, inc=(s_dve, 1))
                            else:
                                mxa = E("dve", lambda e, o=xsl, p=bank:
                                        e.tensor_tensor(o, o, p, ALU.add),
                                        wait=mwo, inc=(s_dve, 1))
                            m_xadd[idx % 8] = mxa
                            m_xadd_last = mxa
                    m_wo_ten = s_ten.mark()
            return dict(ten=s_ten.mark(), act=s_act.mark(), dve=m_xadd_last,
                        scl=s_scl.mark())

        m_gamma_tot = None
        if stage_ge("bl"):
            gamma_pump(66)
            mred = gam["red"]
            for wi in range(3):
                mred = E("dve", lambda e, o=g_part[:, 66 + wi:67 + wi],
                         i=g_part[:, wi * 22:(wi + 1) * 22]:
                         e.tensor_reduce(o, i, mybir.AxisListType.X, ALU.add),
                         inc=(s_dve, 1))
            mgs = dma("syn", g_snd, g_part[:, 66:69], wait=mred)
            mgcc = E("gps", lambda e: e.collective_compute(
                "AllGather", ALU.bypass, replica_groups=RG8,
                ins=[g_snd[:, :]], outs=[g_rcv[:, :, :]]),
                wait=mgs, inc=(s_cc, 1))

        l0_done = ffn0
        if stage_ge("heads"):
            _PHASE[0] = "heads0"
            l0_done = emit_heads(0, [wq0_h[i] for i in range(NQH)],
                                 [wo0_h[i] for i in range(NQH)],
                                 kvr[0], True, True, True, kv0["cc"], ffn0)

        if stage_ge("ln1"):
            _PHASE[0] = "ln1"
            m_xn1_d, m_xn1_a, m_ln1_t = emit_ln("L1", l0_done["dve"], ())
            if _TAPS:
                dma("syn", taps["xnh"], xnh, wait=m_xn1_d, waits=(m_xn1_a,))
                dma("syn", taps["xnl"], xnl, wait=m_xn1_d)
        if stage_ge("bl"):
            for r in range(8):
                mgl = dma("syn", g_all[:, r * 3:r * 3 + 3], g_rcv[r, :, :],
                          waits=((s_cc, mgcc[1]),) if r == 0 else ())
            for i in range(4):
                mga = E("dve", lambda e, o=g_part[:, i * 3:i * 3 + 3],
                        a=g_all[:, 2 * i * 3:2 * i * 3 + 3],
                        b=g_all[:, (2 * i + 1) * 3:(2 * i + 1) * 3 + 3]:
                        e.tensor_tensor(o, a, b, ALU.add),
                        wait=mgl if i == 0 else None, inc=(s_dve, 1))
            for i in range(2):
                mga = E("dve", lambda e, o=g_part[:, 12 + i * 3:12 + i * 3 + 3],
                        a=g_part[:, 2 * i * 3:2 * i * 3 + 3],
                        b=g_part[:, (2 * i + 1) * 3:(2 * i + 1) * 3 + 3]:
                        e.tensor_tensor(o, a, b, ALU.add),
                        wait=mga if i == 0 else None, inc=(s_dve, 1))
            mga = E("dve", lambda e:
                    e.tensor_tensor(g_part[:, 69:72], g_part[:, 12:15],
                                    g_part[:, 15:18], ALU.add),
                    wait=mga, inc=(s_dve, 1))
            m_gamma_tot = mga
            thr = []
            n_el = HID1 * D
            for wi in range(3):
                mt1 = E("ten", lambda e, p=pba[2][0:1, 0:1], c=g_part[:, 69 + wi:70 + wi]:
                        e.matmul(p, c, ones_c32, start=True, stop=True),
                        wait=m_gamma_tot,
                        waits=((s_dve, l0_done["dve"][1]),
                               (s_act, l0_done["act"][1])) if wi == 0 else (),
                        inc=(s_ten, 1))
                mt2 = E("act", lambda e, p=pba[2][0:1, 0:1],
                        o=g_thr[0:1, wi:wi + 1]:
                        e.activation(out=o, in_=p, func=AF.Identity,
                                     scale=0.5 / n_el, bias=c_geps[0:1, :]),
                        wait=mt1, inc=(s_act, 1))
                mt3 = E("ten", lambda e, p=pba[3][:, 0:1], r=g_thr[0:1, wi:wi + 1]:
                        e.matmul(p, ones_r, r, start=True, stop=True),
                        wait=mt2, inc=(s_ten, 1))
                mt4 = E("act", lambda e, o=g_thr[:, wi:wi + 1], p=pba[3][:, 0:1]:
                        e.copy(out=o, in_=p), wait=mt3, inc=(s_act, 1))
                mt5 = E("act", lambda e, o=g_thr[:, 3 + wi:4 + wi], i=g_thr[:, wi:wi + 1]:
                        e.activation(out=o, in_=i, func=AF.Identity, scale=-1.0),
                        wait=mt4, inc=(s_act, 1))
                thr.append((g_thr[:, wi:wi + 1], g_thr[:, 3 + wi:4 + wi], mt5))
            if _TAPS:
                dma("syn", taps["thr"], g_thr, wait=None, waits=(thr[2][2],))
                dma("syn", taps["gp"], g_part, wait=m_gamma_tot)
                dma("syn", taps["ga"], g_all, wait=m_gamma_tot)

            _PHASE[0] = "kv1"
            kv1 = emit_kv(1, [wk1_h[i] for i in range(4)], [wv1_h[i] for i in range(4)],
                          kvs[1], False, False, False, m_xn1_d, m_xn1_a,
                          l0_done["ten"])

            def emit_bitlin(prev):
                A = Bump(a32, 6144, "bl")
                A16 = Bump(a16, 20480, "bl16")
                stg = A.take(2048)
                sig = A.take(2048)
                h32 = A.take(2048)
                cmp16 = A16.take(2048)
                q1c = A16.take(4096)
                q2c = A16.take(4096)
                q3c = A16.take(4096)
                hhi = A16.take(2048)
                hlo = A16.take(2048)

                state = {"stg_dve": prev["dve"]}

                def quantize(dst, src_h, hc, ti, dst_free_ten, extra_ws=()):
                    tpos, tneg, mthr = thr[ti]
                    mq = None
                    for hf in range(2):
                        ws = extra_ws if hf == 0 else ()
                        md = dma("syn", stg, src_h[hc, hf], wait=state["stg_dve"], waits=ws)
                        wsA = ()
                        if hf == 0:
                            wsA = ((s_ten, dst_free_ten[1]), mthr)
                        mA = E("dve", lambda e, o=cmp16, i=stg, t=tneg:
                               e.tensor_scalar(o, i, t, 1.0, ALU.is_lt, ALU.mult),
                               wait=md, waits=wsA, inc=(s_dve, 1))
                        mq = E("dve", lambda e, o=dst[:, hf * 2048:hf * 2048 + 2048],
                               i=stg, t=tpos:
                               e.scalar_tensor_tensor(o, i, t, cmp16, ALU.is_gt, ALU.subtract),
                               wait=mA, inc=(s_dve, 1))
                        state["stg_dve"] = mq
                    return mq

                m_f1 = [prev["dve"]] * 8
                m_f1_last = prev["dve"]
                m_hl_prev = None
                m_q3_used = prev["ten"]
                msig_prev = prev["act"]
                first = True
                pre_q = {}
                for hc in range(NBC):
                    ws0 = ((s_act, prev["act"][1]),) if first else ()
                    if hc == 0:
                        mq1 = quantize(q1c, w1q_h, 0, 0, prev["ten"], ws0)
                        mq2 = quantize(q2c, w2q_h, 0, 1, prev["ten"])
                    else:
                        mq1, mq2 = pre_q[hc]
                    mq3 = quantize(q3c, w3q_h, hc, 2, m_q3_used)
                    mm1 = [None] * 4
                    for hc2 in range(2):
                        for tc in range(2):
                            bank = pba[hc2 * 2 + tc]
                            for dc in range(DCH):
                                la = q1c[:, dc * 256 + hc2 * 128: dc * 256 + hc2 * 128 + 128]
                                rh = xnh[:, dc * TOK + tc * 512: dc * TOK + tc * 512 + 512]
                                rl = xnl[:, dc * TOK + tc * 512: dc * TOK + tc * 512 + 512]
                                w = mq1 if dc == 0 else None
                                ws = ()
                                if hc2 == 0 and tc == 0 and dc == 0:
                                    ws = ((s_act, msig_prev[1]),
                                          (s_dve, m_f1_last[1]))
                                E("ten", lambda e, p=bank, a=la, b=rh, st=(dc == 0):
                                  e.matmul(p, a, b, start=st, stop=False),
                                  wait=w, waits=ws)
                                mm1[hc2 * 2 + tc] = E(
                                    "ten", lambda e, p=bank, a=la, b=rl, sp=(dc == DCH - 1):
                                    e.matmul(p, a, b, start=False, stop=sp),
                                    inc=(s_ten, 1) if dc == DCH - 1 else None)
                    msig = [None] * 4
                    for hc2 in range(2):
                        for tc in range(2):
                            ws = ()
                            if hc2 == 0 and tc == 0 and m_hl_prev:
                                ws = ((s_dve, m_hl_prev[1]),)
                            msig[hc2 * 2 + tc] = E(
                                "act", lambda e,
                                o=sig[:, hc2 * 1024 + tc * 512: hc2 * 1024 + tc * 512 + 512],
                                p=pba[hc2 * 2 + tc]:
                                e.activation(out=o, in_=p, func=AF.Sigmoid),
                                wait=mm1[hc2 * 2 + tc],
                                waits=ws + (((s_scl, kv1["scl"][1]),) if first else ()),
                                inc=(s_act, 1))
                    msig_prev = msig[3]
                    mm2 = [None] * 4
                    for hc2 in range(2):
                        for tc in range(2):
                            bank = pba[4 + hc2 * 2 + tc]
                            for dc in range(DCH):
                                la = q2c[:, dc * 256 + hc2 * 128: dc * 256 + hc2 * 128 + 128]
                                rh = xnh[:, dc * TOK + tc * 512: dc * TOK + tc * 512 + 512]
                                rl = xnl[:, dc * TOK + tc * 512: dc * TOK + tc * 512 + 512]
                                w = mq2 if dc == 0 else None
                                E("ten", lambda e, p=bank, a=la, b=rh, st=(dc == 0):
                                  e.matmul(p, a, b, start=st, stop=False), wait=w)
                                mm2[hc2 * 2 + tc] = E(
                                    "ten", lambda e, p=bank, a=la, b=rl, sp=(dc == DCH - 1):
                                    e.matmul(p, a, b, start=False, stop=sp),
                                    inc=(s_ten, 1) if dc == DCH - 1 else None)
                    m_hl = None
                    for hc2 in range(2):
                        for tc in range(2):
                            sl = slice(hc2 * 1024 + tc * 512, hc2 * 1024 + tc * 512 + 512)
                            mh32 = E("dve", lambda e, o=h32[:, sl], s2=sig[:, sl],
                                     p=pba[4 + hc2 * 2 + tc]:
                                     e.tensor_tensor(o, s2, p, ALU.mult),
                                     wait=mm2[hc2 * 2 + tc],
                                     waits=(msig[hc2 * 2 + tc],), inc=(s_dve, 1))
                            mhh = E("act", lambda e, o=hhi[:, sl], i=h32[:, sl]:
                                    e.copy(out=o, in_=i), wait=mh32,
                                    waits=(((s_scl, kv1["scl"][1]),) if first else ()),
                                    inc=(s_act, 1))
                            m_hl = E("dve", lambda e, o=hlo[:, sl], i=h32[:, sl],
                                     h2=hhi[:, sl]:
                                     e.tensor_tensor(o, i, h2, ALU.subtract),
                                     wait=mhh, inc=(s_dve, 1))
                    m_hl_prev = m_hl
                    first = False
                    for oc in range(16):
                        for tc in range(2):
                            idx = oc * 2 + tc
                            bank = pba[idx % 8]
                            cnt = 0
                            for pc in range(2):
                                la = q3c[:, pc * 2048 + oc * 128: pc * 2048 + oc * 128 + 128]
                                rh = hhi[:, pc * 1024 + tc * 512: pc * 1024 + tc * 512 + 512]
                                rl = hlo[:, pc * 1024 + tc * 512: pc * 1024 + tc * 512 + 512]
                                w = None
                                ws = ()
                                if pc == 0:
                                    w = mq3
                                    ws = (m_hl,) if idx == 0 else ()
                                    if idx >= 8:
                                        ws = ws + ((s_dve, m_f1[idx % 8][1]),)
                                E("ten", lambda e, p=bank, a=la, b=rh, st=(pc == 0):
                                  e.matmul(p, a, b, start=st, stop=False),
                                  wait=w, waits=ws)
                                E("ten", lambda e, p=bank, a=la, b=rl, sp=(pc == 1):
                                  e.matmul(p, a, b, start=False, stop=sp),
                                  inc=(s_ten, 1) if pc == 1 else None)
                            mmk = s_ten.mark()
                            xsl = xb[:, oc * TOK + tc * 512: oc * TOK + tc * 512 + 512]
                            m_f1[idx % 8] = E("dve", lambda e, o=xsl, p=bank:
                                              e.tensor_tensor(o, o, p, ALU.add),
                                              wait=mmk, inc=(s_dve, 1))
                            m_f1_last = m_f1[idx % 8]
                    m_q3_used = s_ten.mark()
                    if hc + 1 < NBC:
                        pre_q[hc + 1] = (
                            quantize(q1c, w1q_h, hc + 1, 0, mm1[3]),
                            quantize(q2c, w2q_h, hc + 1, 1, mm2[3]))
                return dict(ten=s_ten.mark(), act=s_act.mark(), dve=m_f1_last,
                            scl=s_scl.mark())

            _PHASE[0] = "bl"
            bl = emit_bitlin(kv1)
        if stage_ge("full"):
            _PHASE[0] = "heads1"
            l1_done = emit_heads(1, [wq1_h[i] for i in range(NQH)],
                                 [wd1_h[i] for i in range(NQH)],
                                 kvr[1], False, False, False, kv1["cc"], bl)
            fin = l1_done
        else:
            fin = l0_done if l0_done is not None else dict(
                ten=s_ten.mark(), act=s_act.mark(), dve=s_dve.mark(), scl=s_scl.mark())

        m_fin_d = s_dve.mark()
        m_fin_a = s_act.mark()
        m_fin_t = s_ten.mark()
        mo = dma("syn", out_h, xb, wait=m_fin_d,
                 waits=tuple(m for m in (m_fin_a, m_fin_t) if m[1] > 0))

        with nc.Block() as block:
            @block.sync
            def _(sync):
                P.run("syn", sync)
                sync.wait_ge(s_syn.h, mo[1])

            @block.tensor
            def _(tensor):
                P.run("ten", tensor)

            @block.scalar
            def _(scalar):
                P.run("act", scalar)

            @block.vector
            def _(vector):
                P.run("dve", vector)

            @block.gpsimd
            def _(gpsimd):
                P.run("gps", gpsimd)

    return nc


def _rope_tables(tok0):
    i = np.arange(1, 65, dtype=np.float32)
    thetas = (np.float32(10000.0) ** ((np.float32(-2.0) * (i - np.float32(1.0)))
                                      / np.float32(128.0))).astype(np.float32)
    pos = np.arange(tok0 + 1, tok0 + TOK + 1, dtype=np.float32)
    ang = (pos[:, None] * thetas[None, :]).astype(np.float32)
    ang64 = ang.astype(np.float64)
    cos = np.cos(ang64).astype(np.float32)
    sin = np.sin(ang64).astype(np.float32)
    ctab = np.empty((128, TOK), np.float32)
    stab = np.empty((128, TOK), np.float32)
    ctab[0:64] = cos.T
    ctab[64:128] = cos.T
    stab[0:64] = -sin.T
    stab[64:128] = sin.T
    return ctab, stab


def _pack_bias(b, nch):
    return np.ascontiguousarray(np.asarray(b, np.float32).reshape(nch, 128).T)


_W_CACHE = {}


def _prep_weights(inputs):
    key = id(inputs.get("L0_attn_Wq"))
    if key in _W_CACHE:
        return _W_CACHE[key]
    f = np.float32
    cc = np.ascontiguousarray
    w = {}
    WqT = np.asarray(inputs["L0_attn_Wq"], f).T
    w["wq0"] = cc(WqT.reshape(16, 128, 16, 128).transpose(2, 1, 0, 3).reshape(16, 128, 2048))
    w["wk0"] = cc(np.asarray(inputs["L0_attn_Wk"], f).T
                  .reshape(4, 4, 128, 512).transpose(0, 2, 1, 3).reshape(4, 128, 2048))
    w["wv0"] = cc(np.asarray(inputs["L0_attn_Wv"], f).T
                  .reshape(4, 4, 128, 512).transpose(0, 2, 1, 3).reshape(4, 128, 2048))
    w["wo0"] = cc(np.asarray(inputs["L0_attn_Wo"], f).T.reshape(16, 128, 2048))
    w["w00"] = cc(np.asarray(inputs["L0_ffn_W0"], f).T
                  .reshape(2, 8, 128, 16, 512).transpose(3, 0, 2, 1, 4).reshape(16, 2, 128, 4096))
    w["w10"] = cc(np.asarray(inputs["L0_ffn_W1"], f).T
                  .reshape(16, 2, 2, 128, 2048).transpose(0, 1, 3, 2, 4).reshape(16, 2, 128, 4096))
    w["wq1"] = cc(np.asarray(inputs["L1_attn_w_q"], f)
                  .reshape(16, 128, 16, 128).transpose(2, 1, 0, 3).reshape(16, 128, 2048))
    wkv = np.asarray(inputs["L1_attn_w_kv"], f)
    w["wk1"] = cc(wkv[:, 0:512].reshape(4, 4, 128, 512).transpose(0, 2, 1, 3).reshape(4, 128, 2048))
    w["wv1"] = cc(wkv[:, 512:1024].reshape(4, 4, 128, 512).transpose(0, 2, 1, 3).reshape(4, 128, 2048))
    w["wd1"] = cc(np.asarray(inputs["L1_attn_w_dense"], f).reshape(16, 128, 2048))
    w1 = np.asarray(inputs["L1_ffn_w1"], f)
    w2 = np.asarray(inputs["L1_ffn_w2"], f)
    w3 = np.asarray(inputs["L1_ffn_w3"], f)
    w["w1q"] = cc(w1.T.reshape(2, 8, 128, 22, 256).transpose(3, 0, 2, 1, 4).reshape(22, 2, 128, 2048))
    w["w2q"] = cc(w2.T.reshape(2, 8, 128, 22, 256).transpose(3, 0, 2, 1, 4).reshape(22, 2, 128, 2048))
    w["w3q"] = cc(w3.T.reshape(22, 2, 128, 2048))
    w["bq_pk"] = _pack_bias(inputs["L0_attn_bq"], 16)
    w["bk_pk"] = _pack_bias(inputs["L0_attn_bk"], 4)
    w["bv_b"] = cc(np.broadcast_to(np.asarray(inputs["L0_attn_bv"], f), (128, 512)))
    w["bo_pk"] = _pack_bias(inputs["L0_attn_bo"], 16)
    w["b0_pk"] = _pack_bias(inputs["L0_ffn_bias0"], 64)
    w["b1_pk"] = _pack_bias(inputs["L0_ffn_bias1"], 16)
    w["kidx"] = cc((np.arange(16)[None, :] * 128 + np.arange(128)[:, None]).astype(f))
    N8 = HID1 * D // 8
    w["_gsrc"] = []
    for c in range(8):
        parts = [m.ravel()[c * N8:(c + 1) * N8].reshape(22, 128, 512)
                 for m in (w1, w2, w3)]
        w["_gsrc"].append(cc(np.concatenate(parts, axis=0)))
    _W_CACHE[key] = w
    return w


def kernel(**inputs):
    w = _prep_weights(inputs)
    x = np.asarray(inputs["L0_attn_x"], dtype=np.float32)
    base = {k: v for k, v in w.items() if not k.startswith("_")}
    in_maps = []
    for c in range(8):
        b, half = c // 2, c % 2
        tok0 = half * TOK
        xs = np.ascontiguousarray(
            x[b, tok0:tok0 + TOK, :].T.reshape(16, 128, TOK).transpose(1, 0, 2)
            .reshape(128, 16 * TOK))
        ctab, stab = _rope_tables(tok0)
        m = dict(base)
        m["x0t"] = xs
        m["qidxb"] = np.ascontiguousarray(
            np.broadcast_to(np.arange(tok0, tok0 + TOK, dtype=np.float16), (128, TOK)))
        m["ropecos"] = ctab
        m["ropesin"] = stab
        m["gsrc"] = w["_gsrc"][c]
        in_maps.append(m)
    nc = build_nc()
    res = run_bass_kernel_spmd(nc, in_maps, core_ids=list(range(8)))
    out = np.empty((B, T, D), dtype=np.float32)
    for c in range(8):
        b, half = c // 2, c % 2
        o = np.asarray(res.results[c]["out"])
        out[b, half * TOK:(half + 1) * TOK, :] = (
            o.reshape(128, 16, TOK).transpose(1, 0, 2).reshape(D, TOK).T)
    return out
